# version 04
# baseline (speedup 1.0000x reference)
import contextlib
from collections import deque

import numpy as np
import ml_dtypes
import concourse.bass as bass
import concourse.mybir as mybir
from concourse.bass_utils import run_bass_kernel_spmd

F32 = mybir.dt.float32
BF16 = mybir.dt.bfloat16
AF = mybir.ActivationFunctionType
ALU = mybir.AluOpType
AX = mybir.AxisListType

C = 2048
NQT = 16
NH = 16
HS = 2816
HE = 1408
NE = 8
SCALE = 1.0 / float(np.sqrt(128.0))
NEG = -1.0e30
EPS = 1e-5

PUMP_B = True
ENGS = ("pe", "act", "dve", "pool", "sp")
QUEUES = {"dsp": "sp", "dact": "act", "dpool": "pool"}

WB_ALL = ["wbA", "wbB", "wbSS", "dn0", "dn1", "ustg0", "ustg1"]


class Rec:

    def __init__(self):
        self.ops = {e: [] for e in ENGS}
        self.cnt = {e: 0 for e in ENGS}
        self.dcnt = {q: 0 for q in QUEUES}
        self.ccnt = 0
        self.know = {e: {} for e in ENGS}
        self.reg = {}
        self.prep_q = deque()

    def _region(self, r):
        if r not in self.reg:
            self.reg[r] = {"w": {}, "r": {}}
        return self.reg[r]

    def _emit(self, eng, fn, reads, writes, sem, val, skip):
        waits = {}
        for r in reads:
            for s, v in self._region(r)["w"].items():
                if s not in skip and self.know[eng].get(s, 0) < v:
                    waits[s] = max(waits.get(s, 0), v)
        for r in writes:
            rr = self._region(r)
            for s, v in rr["w"].items():
                if s not in skip and self.know[eng].get(s, 0) < v:
                    waits[s] = max(waits.get(s, 0), v)
            for s, v in rr["r"].items():
                if s not in skip and self.know[eng].get(s, 0) < v:
                    waits[s] = max(waits.get(s, 0), v)
        for s, v in waits.items():
            self.know[eng][s] = v
        self.ops[eng].append((sorted(waits.items()), fn, sem, val, (list(reads), list(writes))))
        for r in writes:
            rr = self._region(r)
            rr["w"] = {sem: val}
            rr["r"] = {}
        for r in reads:
            self._region(r)["r"][sem] = val

    def op(self, eng, fn, reads=(), writes=()):
        self.cnt[eng] += 1
        self._emit(eng, fn, reads, writes, eng, self.cnt[eng],
                   (eng,) if eng == "pe" else ())

    def dma(self, q, fn, reads=(), writes=()):
        self.dcnt[q] += 16
        self._emit(QUEUES[q], fn, reads, writes, q, self.dcnt[q], ())

    def coll(self, fn, reads=(), writes=()):
        self.ccnt += 1
        self._emit("pool", fn, reads, writes, "cc", self.ccnt, ())

    def wait_only(self, eng, sems):
        waits = []
        for s, v in sems.items():
            if v > 0 and self.know[eng].get(s, 0) < v:
                waits.append((s, v))
                self.know[eng][s] = v
        if waits:
            self.ops[eng].append((waits, None, None, 0, ([], [])))

    def barrier(self):
        engines = ("pe", "act", "dve", "sp")
        cur = {e: self.cnt[e] for e in engines}
        cur["dsp"] = self.dcnt["dsp"]
        for e in engines:
            self.wait_only(e, {s: v for s, v in cur.items() if s != e})

    def pump(self, n):
        for _ in range(n):
            if not self.prep_q:
                return
            self.prep_q.popleft()()

    def pump_all(self):
        while self.prep_q:
            self.prep_q.popleft()()


def build_nc(n_sub=4):
    nc = bass.Bass("TRN2", target_bir_lowering=False)
    rec = Rec()

    x_in = nc.dram_tensor("x", [NQT, 128, C], F32, kind="ExternalInput").ap()
    out_d = nc.dram_tensor("out", [NQT, 128, C], F32, kind="ExternalOutput").ap()
    aW = {nm: nc.dram_tensor(nm, [C, C], F32, kind="ExternalInput").ap()
          for nm in ("aWQ", "aWK", "aWV", "aWO")}
    ab = {nm: nc.dram_tensor(nm, [C], F32, kind="ExternalInput").ap()
          for nm in ("aWQb", "aWKb", "aWVb", "aWOb")}
    bW = {nm: nc.dram_tensor(nm, [C, C], F32, kind="ExternalInput").ap()
          for nm in ("bwq", "bwk", "bwv", "bwo")}
    bbo = nc.dram_tensor("bbo", [C], F32, kind="ExternalInput").ap()
    gw_in = nc.dram_tensor("gw", [NE, C], F32, kind="ExternalInput").ap()
    gb_row = nc.dram_tensor("gb_row", [1, 128], F32, kind="ExternalInput").ap()
    sgw_in = nc.dram_tensor("sgw", [2 * HS, C], F32, kind="ExternalInput").ap()
    sdw_in = nc.dram_tensor("sdw", [C, HS], F32, kind="ExternalInput").ap()
    upw_in = nc.dram_tensor("upw", [NE, 2 * HE, C], F32, kind="ExternalInput").ap()
    dnw_in = nc.dram_tensor("dnw", [NE, C, HE], F32, kind="ExternalInput").ap()
    idn_in = nc.dram_tensor("idn", [128, 128], BF16, kind="ExternalInput").ap()
    onesb_in = nc.dram_tensor("ones_bf", [1, 128], BF16, kind="ExternalInput").ap()
    onesf_in = nc.dram_tensor("ones_f", [1, 128], F32, kind="ExternalInput").ap()
    mask_in = nc.dram_tensor("mask2", [128, 256], F32, kind="ExternalInput").ap()

    scr = {}
    for nm in ("aWQ", "aWK"):
        scr[nm] = nc.dram_tensor("s_" + nm, [16, 128, C], BF16).ap()
    for nm in ("aWV", "aWO", "bwq", "bwk", "bwv", "bwo"):
        scr[nm] = nc.dram_tensor("s_" + nm, [C, C], BF16).ap()
    scr["sgw"] = nc.dram_tensor("s_sgw", [2 * HS // 128, 128, C], BF16).ap()
    scr["sdw"] = nc.dram_tensor("s_sdw", [HS, C], BF16).ap()
    for e in range(NE):
        scr[f"up{e}"] = nc.dram_tensor(f"s_up{e}", [2 * HE // 128, 128, C], BF16).ap()
        scr[f"dn{e}"] = nc.dram_tensor(f"s_dn{e}", [HE, C], BF16).ap()
    cc_in = nc.dram_tensor("cc_in", [2, C, C], BF16)
    cc_out = nc.dram_tensor("cc_out", [2, 2, C, C], BF16)
    q_dram = nc.dram_tensor("q_dram", [C, C], BF16).ap()
    ctx_dram = nc.dram_tensor("ctx_dram", [C, C], BF16).ap()

    ctx = contextlib.ExitStack()
    sb = lambda name, shape, dt: ctx.enter_context(nc.sbuf_tensor(name, shape, dt))

    with ctx:
        x_res = sb("x_res", [128, NQT * C], BF16)
        arena = sb("arena", [128, NQT * C], BF16)
        ar32 = arena[:, :].bitcast(F32)

        acc = sb("acc", [128, 4 * C], F32)
        h_sb = sb("h_sb", [128, 11 * 512], BF16)
        wbuf = sb("wbuf", [128, 8192], BF16)
        outst = sb("outst", [128, C], F32)
        xnt = wbuf[:, 4096:6144]
        gwb = wbuf[0:8, 0:C]
        outst_bf = outst[:, :].bitcast(BF16)
        idn = sb("sb_idn", [128, 128], BF16)
        mask2 = sb("sb_mask2", [128, 256], F32)
        stats = sb("stats", [128, 96], F32)
        nmx = sb("nmx", [128, 4], F32)
        sall = sb("sall", [128, 128], F32)
        weall = sb("weall", [128, 128], F32)
        rt = sb("rt", [128, 384], F32)
        seg = sb("seg", [128, 64], F32)
        bqc = sb("bqc", [128, 16], F32)
        bkc = sb("bkc", [128, 16], F32)
        brows = sb("brows", [128, C], BF16)
        brow_f = outst[0:1, 0:C]
        brow_bf = brows[0:1, :]
        ones_bf = sb("sb_onesb", [128, 128], BF16)[0:1, :]
        ones_f = sb("sb_onesf", [128, 128], F32)[0:1, :]
        gb_bc = sb("gb_bc", [128, 128], F32)
        consts = sb("consts", [128, 4], F32)
        gwT = sb("gwT", [128, 128], BF16)

        psum = {}
        for nm in ("ps0", "ps1", "psc0", "psc1"):
            psum[nm] = ctx.enter_context(nc.psum_tensor(nm, [128, 512], F32))
        for nm in ("pstr0", "pstr1", "psp0", "psp1"):
            psum[nm] = ctx.enter_context(nc.psum_tensor(nm, [128, 1024], BF16))

        acc_bf = acc[:, :].bitcast(BF16)
        stgf = [acc[:, 0:2048], acc[:, 2048:4096]]
        stgb = [acc_bf[:, 8192:10240], acc_bf[:, 10240:12288]]
        sstgf = outst[:, :]
        sstgb = brows[:, :]

        ACC_ALL = ["acc", "stgf0", "stgf1", "stgb0", "stgb1"]

        def mm(bname, out_ap, lhsT, rhs, start, stop, reads):
            rec.op("pe", lambda t, o=out_ap, l=lhsT, r=rhs, s=start, p=stop:
                   t.matmul(o, l, r, start=s, stop=p),
                   reads=list(reads), writes=[bname])

        def tr(bname, out_ap, in_ap, reads, np_=128):
            rec.op("pe", lambda t, o=out_ap, i=in_ap, n=np_:
                   t.transpose(o, i, idn[0:n, 0:n]),
                   reads=list(reads) + ["idn"], writes=[bname])

        rec.op("dve", lambda v: v.memset(consts[:, 1:2], SCALE), writes=["consts"])
        rec.dma("dsp", lambda e: e.dma_start(out=idn[:, :], in_=idn_in), writes=["idn"])
        rec.dma("dsp", lambda e: e.dma_start(out=mask2[:, :], in_=mask_in), writes=["mask2"])
        rec.dma("dsp", lambda e: e.dma_start(out=ones_bf, in_=onesb_in), writes=["onesb"])
        rec.dma("dsp", lambda e: e.dma_start(out=ones_f, in_=onesf_in), writes=["onesf"])
        rec.dma("dsp", lambda e: e.dma_start(
            out=bqc[:, :], in_=ab["aWQb"].rearrange("(k p) -> p k", p=128),
            allow_slow_non_contiguous=True), writes=["bqc"])
        rec.dma("dsp", lambda e: e.dma_start(
            out=bkc[:, :], in_=ab["aWKb"].rearrange("(k p) -> p k", p=128),
            allow_slow_non_contiguous=True), writes=["bkc"])
        def fill_brow(src_ap):
            rec.dma("dsp", lambda e, s=src_ap: e.dma_start(
                out=brow_f, in_=s.rearrange("(o c) -> o c", o=1)),
                writes=["browf", "outst", "kvstg0", "kvstg1"])
            rec.op("dve", lambda v: v.tensor_copy(brow_bf, brow_f),
                   reads=["browf"], writes=["browbf"])
        rec.dma("dsp", lambda e: e.dma_start(out=brow_f[:, 0:128], in_=gb_row),
                writes=["browf", "outst", "kvstg0", "kvstg1"])
        rec.op("pe", lambda t: t.matmul(psum["ps0"][:, 0:128], ones_f,
                                        brow_f[:, 0:128], start=True, stop=True),
               reads=["onesf", "browf"], writes=["ps0"])
        rec.op("dve", lambda v: v.tensor_copy(gb_bc[:, :], psum["ps0"][:, 0:128]),
               reads=["ps0"], writes=["gbbc"])
        rec.dma("dsp", lambda e: e.dma_start(out=stgf[0][0:8, 0:C], in_=gw_in),
                writes=["stgf0"])
        rec.op("pool", lambda g: g.tensor_copy(gwb[:, :], stgf[0][0:8, 0:C]),
               reads=["stgf0"], writes=["gwb", "wbA"])
        for k in range(16):
            tr("pstr0", psum["pstr0"][:, k * 8:(k + 1) * 8],
               gwb[0:8, k * 128:(k + 1) * 128], ["gwb", "wbA"], np_=8)
        rec.op("act", lambda s: s.copy(out=gwT[:, :], in_=psum["pstr0"][:, 0:128]),
               reads=["pstr0"], writes=["gwT"])

        for i in range(NQT):
            rec.dma("dsp", lambda e, i=i: e.dma_start(out=outst[:, :], in_=x_in[i]),
                    writes=["outst", "kvstg0", "kvstg1", "browf"])
            rec.op("dve", lambda v, i=i: v.tensor_copy(
                x_res[:, i * C:(i + 1) * C], outst[:, :]),
                reads=["outst"], writes=[f"xres{i}"])

        def prep_t3_item(w_ap, scr3_ap, scrkey, fi, slot):
            f_ap, b_ap = stgf[slot], stgb[slot]
            fkeys = [f"stgf{slot}"]
            bkeys = [f"stgb{slot}"]
            rec.dma("dact", lambda e, w=w_ap, fi=fi, f=f_ap: e.dma_start(
                out=f[:, :], in_=w[fi * 128:(fi + 1) * 128, :]), writes=fkeys)
            rec.op("pool", lambda g, f=f_ap, b=b_ap: g.tensor_copy(
                b[:, :], f[:, :]), reads=fkeys, writes=bkeys)
            for g in range(2):
                bn = f"psp{g}"
                for t in range(8):
                    k = g * 8 + t
                    tr(bn, psum[bn][:, t * 128:(t + 1) * 128],
                       b_ap[:, k * 128:(k + 1) * 128], bkeys)
                rec.op("act", lambda s, bn=bn, a=g * 1024, b=b_ap:
                       s.copy(out=b[:, a:a + 1024], in_=psum[bn][:, :]),
                       reads=[bn], writes=bkeys)
            rec.dma("dact", lambda e, sc=scr3_ap, fi=fi, b=b_ap: e.dma_start(
                out=sc[fi], in_=b[:, :]), reads=bkeys, writes=[scrkey])

        def prep_col_item(w_ap, scr_ap, scrkey, F, k, slot):
            f_ap, b_ap = stgf[slot], stgb[slot]
            fkeys = [f"stgf{slot}"]
            bkeys = [f"stgb{slot}"]
            rec.dma("dact", lambda e, w=w_ap, k=k, f=f_ap, n=F: e.dma_start(
                out=f[:, 0:n].rearrange("p (fc c) -> p fc c", c=128),
                in_=w[:, k * 128:(k + 1) * 128].rearrange("(fc p) c -> p fc c", p=128)),
                writes=fkeys)
            rec.op("pool", lambda g, f=f_ap, b=b_ap, n=F: g.tensor_copy(
                b[:, 0:n], f[:, 0:n]), reads=fkeys, writes=bkeys)
            for g in range((F // 128 + 7) // 8):
                t0, t1 = g * 8, min(g * 8 + 8, F // 128)
                bn = f"psp{g % 2}"
                for t in range(t0, t1):
                    tr(bn, psum[bn][:, (t - t0) * 128:(t - t0 + 1) * 128],
                       b_ap[:, t * 128:(t + 1) * 128], bkeys)
                rec.op("act", lambda s, bn=bn, a=t0 * 128, w=(t1 - t0) * 128, b=b_ap:
                       s.copy(out=b[:, a:a + w], in_=psum[bn][:, 0:w]),
                       reads=[bn], writes=bkeys)
            rec.dma("dact", lambda e, sc=scr_ap, k=k, b=b_ap, n=F: e.dma_start(
                out=sc[k * 128:(k + 1) * 128, :], in_=b[:, 0:n]),
                reads=bkeys, writes=[scrkey])

        def prep_natural_item(w_ap, scr_ap, scrkey, F, k):
            rec.dma("dact", lambda e, w=w_ap, k=k: e.dma_start(
                out=sstgf[:, 0:F], in_=w[k * 128:(k + 1) * 128, :]),
                writes=["outst", "kvstg0", "kvstg1", "browf"])
            rec.op("pool", lambda g: g.tensor_copy(
                sstgb[:, 0:F], sstgf[:, 0:F]), reads=["outst"], writes=["browbf"])
            rec.dma("dact", lambda e, sc=scr_ap, k=k: e.dma_start(
                out=sc[k * 128:(k + 1) * 128, :], in_=sstgb[:, 0:F]),
                reads=["browbf"], writes=[scrkey])

        def attn_a_items():
            its = []
            ns = [0]
            for nm in ("aWQ", "aWK"):
                for fi in range(16):
                    its.append(lambda nm=nm, fi=fi, s=ns[0] % 2: prep_t3_item(
                        aW[nm], scr[nm], "scr_" + nm, fi, s))
                    ns[0] += 1
            for nm in ("aWV", "aWO"):
                for k in range(16):
                    its.append(lambda nm=nm, k=k, s=ns[0] % 2: prep_col_item(
                        aW[nm], scr[nm], "scr_" + nm, C, k, s))
                    ns[0] += 1
            return its

        def ffn_items():
            its = []
            ns = [0]
            for fi in range(2 * HS // 128):
                its.append(lambda fi=fi, s=ns[0] % 2: prep_t3_item(
                    sgw_in, scr["sgw"], "scr_sgw", fi, s))
                ns[0] += 1
            for k in range(HS // 128):
                its.append(lambda k=k, s=ns[0] % 2: prep_col_item(
                    sdw_in, scr["sdw"], "scr_sdw", C, k, s))
                ns[0] += 1
            for e in range(NE):
                for fi in range(2 * HE // 128):
                    its.append(lambda e=e, fi=fi, s=ns[0] % 2: prep_t3_item(
                        upw_in[e], scr[f"up{e}"], f"scr_up{e}", fi, s))
                    ns[0] += 1
                for k in range(HE // 128):
                    its.append(lambda e=e, k=k, s=ns[0] % 2: prep_col_item(
                        dnw_in[e], scr[f"dn{e}"], f"scr_dn{e}", C, k, s))
                    ns[0] += 1
            return its

        def attn_b_items():
            its = []
            for nm in ("bwq", "bwk", "bwv", "bwo"):
                for k in range(16):
                    its.append(lambda nm=nm, k=k: prep_natural_item(
                        bW[nm], scr[nm], "scr_" + nm, C, k))
            return its

        rec.prep_q.extend(attn_a_items())
        rec.pump_all()
        if n_sub >= 2:
            rec.prep_q.extend(ffn_items())

        def ln_phase():
            for i in range(NQT):
                xr = x_res[:, i * C:(i + 1) * C]
                rec.op("dve", lambda v, x=xr, i=i: v.reduce_sum(
                    stats[:, i:i + 1], x, axis=AX.X),
                    reads=[f"xres{i}"], writes=["st_sum"])
                rec.op("act", lambda s, x=xr, i=i: s.activation(
                    out=xnt[:, :], in_=x, func=AF.Square,
                    accum_out=stats[:, 16 + i:17 + i]),
                    reads=[f"xres{i}"], writes=["xnt", "dn0", "dn1", "ustg0", "ustg1", "st_sq"])
            rec.op("dve", lambda v: v.tensor_scalar_mul(
                stats[:, 32:48], stats[:, 0:16], 1.0 / C),
                reads=["st_sum"], writes=["st_mean"])
            rec.op("dve", lambda v: v.tensor_scalar_mul(
                stats[:, 48:64], stats[:, 16:32], 1.0 / C),
                reads=["st_sq"], writes=["st_var"])
            rec.op("dve", lambda v: v.tensor_mul(
                stats[:, 0:16], stats[:, 32:48], stats[:, 32:48]),
                reads=["st_mean"], writes=["st_sum"])
            rec.op("dve", lambda v: v.tensor_sub(
                stats[:, 48:64], stats[:, 48:64], stats[:, 0:16]),
                reads=["st_sum", "st_var"], writes=["st_var"])
            rec.op("dve", lambda v: v.tensor_scalar_add(
                stats[:, 48:64], stats[:, 48:64], EPS),
                reads=["st_var"], writes=["st_var"])
            rec.op("act", lambda s: s.activation(
                out=stats[:, 48:64], in_=stats[:, 48:64], func=AF.Ln),
                reads=["st_var"], writes=["st_var"])
            rec.op("dve", lambda v: v.tensor_scalar_mul(
                stats[:, 48:64], stats[:, 48:64], -0.5),
                reads=["st_var"], writes=["st_var"])
            rec.op("act", lambda s: s.activation(
                out=stats[:, 64:80], in_=stats[:, 48:64], func=AF.Exp),
                reads=["st_var"], writes=["st_rstd"])
            rec.op("dve", lambda v: v.scalar_tensor_tensor(
                stats[:, 80:96], stats[:, 32:48], -1.0, stats[:, 64:80],
                op0=ALU.mult, op1=ALU.mult),
                reads=["st_mean", "st_rstd"], writes=["st_nmr"])
            for i in range(NQT):
                xr = x_res[:, i * C:(i + 1) * C]
                rec.op("act", lambda s, x=xr, i=i: s.activation(
                    out=xnt[:, :], in_=x, func=AF.Identity,
                    bias=stats[:, 80 + i:81 + i], scale=stats[:, 64 + i:65 + i]),
                    reads=[f"xres{i}", "st_rstd", "st_nmr"], writes=["xnt", "dn0", "dn1", "ustg0", "ustg1"])
                for g in range(2):
                    bn = f"pstr{g}"
                    for t in range(8):
                        k = g * 8 + t
                        tr(bn, psum[bn][:, t * 128:(t + 1) * 128],
                           xnt[:, k * 128:(k + 1) * 128], ["xnt"])
                    rec.op("act", lambda s, bn=bn, g=g, i=i: s.copy(
                        out=arena[:, :].rearrange("p (k t) -> p k t", k=16)
                        [:, g * 8:(g + 1) * 8, i * 128:(i + 1) * 128],
                        in_=psum[bn][:, :].rearrange("p (k t) -> p k t", k=8)),
                        reads=[bn], writes=["xnf"])

        def load_lhsT_set(dst_ap, dstkeys, scr3_ap, fc):
            rec.dma("dsp", lambda e, d=dst_ap, s=scr3_ap, fc=fc: e.dma_start(
                out=d, in_=s[fc]), writes=list(dstkeys))

        def load_lhsT_set_plain(dst_ap, dstkeys, scr_ap, fc):
            rec.dma("dsp", lambda e, d=dst_ap, s=scr_ap, fc=fc: e.dma_start(
                out=d.rearrange("p (k c) -> p k c", c=128),
                in_=s.rearrange("(k p) f -> p k f", p=128)[:, :, fc * 128:(fc + 1) * 128]),
                writes=list(dstkeys))

        def load_set_slab(dst_ap, dstkeys, scr_ap, f0, w):
            rec.dma("dsp", lambda e, d=dst_ap, s=scr_ap, f0=f0, w=w: e.dma_start(
                out=d.rearrange("p (k c) -> p k c", c=w),
                in_=s.rearrange("(k p) f -> p k f", p=128)[:, :, f0:f0 + w]),
                writes=list(dstkeys))

        def attn(kind, last):
            scr_q = scr["aWQ" if kind == "a" else "bwq"]
            scr_k = scr["aWK" if kind == "a" else "bwk"]
            scr_v = scr["aWV" if kind == "a" else "bwv"]
            scr_o = scr["aWO" if kind == "a" else "bwo"]
            kq = "scr_" + ("aWQ" if kind == "a" else "bwq")
            kk = "scr_" + ("aWK" if kind == "a" else "bwk")
            kv = "scr_" + ("aWV" if kind == "a" else "bwv")
            ko = "scr_" + ("aWO" if kind == "a" else "bwo")

            rec.barrier()
            ln_phase()

            kvstg = [outst_bf[:, 0:512], outst_bf[:, 512:1024]]
            wslot = [wbuf[:, 0:2048], wbuf[:, 2048:4096]]

            for proj, s_ap, s_key, bias, dest, destkey in (
                    ("k", scr_k, kk, bkc if kind == "a" else None, cc_in.ap()[0], "ccin"),
                    ("q", scr_q, kq, bqc if kind == "a" else None, q_dram, "qdram")):
                for dc in range(16):
                    sl = dc % 2
                    if kind == "a":
                        load_lhsT_set(wslot[sl], [f"wb{'AB'[sl]}"], s_ap, dc)
                    else:
                        load_lhsT_set_plain(wslot[sl], [f"wb{'AB'[sl]}"], s_ap, dc)
                    if dc % 4 == 0:
                        rec.pump(2)
                    for tch in range(4):
                        bn = f"ps{(dc * 4 + tch) % 2}"
                        for k in range(16):
                            mm(bn, psum[bn][:, :], wslot[sl][:, k * 128:(k + 1) * 128],
                               arena[:, k * C + tch * 512:k * C + tch * 512 + 512],
                               k == 0, k == 15, [f"wb{'AB'[sl]}", "xnf", s_key])
                        st = (dc * 4 + tch) % 2
                        if bias is not None:
                            rec.op("act", lambda s, bn=bn, st=st, dc=dc, b=bias:
                                   s.activation(out=kvstg[st][:, :], in_=psum[bn][:, :],
                                                func=AF.Identity, bias=b[:, dc:dc + 1]),
                                   reads=[bn, "bqc", "bkc"],
                                   writes=[f"kvstg{st}", "browf"])
                        else:
                            rec.op("act", lambda s, bn=bn, st=st:
                                   s.copy(out=kvstg[st][:, :], in_=psum[bn][:, :]),
                                   reads=[bn], writes=[f"kvstg{st}", "browf"])
                        rec.dma("dsp", lambda e, d=dest, dc=dc, tch=tch, st=st:
                                e.dma_start(out=d[dc * 128:(dc + 1) * 128,
                                                  tch * 512:(tch + 1) * 512],
                                            in_=kvstg[st][:, :]),
                                reads=[f"kvstg{st}"], writes=[destkey])

            if kind == "a":
                fill_brow(ab["aWVb"])
            for ds in range(4):
                load_set_slab(wbuf[:, 0:8192], WB_ALL, scr_v, ds * 512, 512)
                rec.pump(4)
                for tt in range(16):
                    bn = f"ps{(ds * 16 + tt) % 2}"
                    for k in range(16):
                        mm(bn, psum[bn][:, :],
                           arena[:, k * C + tt * 128:k * C + tt * 128 + 128],
                           wbuf[:, k * 512:(k + 1) * 512],
                           k == 0, k == 15 and kind != "a", WB_ALL + ["xnf", kv])
                    if kind == "a":
                        mm(bn, psum[bn][:, :], ones_bf,
                           brow_bf[:, ds * 512:(ds + 1) * 512],
                           False, True, ["onesb", "browbf"])
                    st = (ds * 16 + tt) % 2
                    rec.op("act", lambda s, bn=bn, st=st:
                           s.copy(out=kvstg[st][:, :], in_=psum[bn][:, :]),
                           reads=[bn], writes=[f"kvstg{st}", "browf"])
                    rec.dma("dsp", lambda e, tt=tt, ds=ds, st=st:
                            e.dma_start(out=cc_in.ap()[1][tt * 128:(tt + 1) * 128,
                                                          ds * 512:(ds + 1) * 512],
                                        in_=kvstg[st][:, :]),
                            reads=[f"kvstg{st}"], writes=["ccin"])

            rec.coll(lambda g: g.collective_compute(
                "AllGather", ALU.bypass,
                replica_groups=[[0, 1], [2, 3], [4, 5], [6, 7]],
                ins=[cc_in.ap().opt()], outs=[cc_out.ap().opt()]),
                reads=["ccin"], writes=["ccout"])

            rec.barrier()

            S_sb = ar32[:, 0:4096]
            P_sb = arena[:, 8192:12288]
            PT_sb = arena[:, 12288:16384]
            Kh = arena[:, 16384:20480]
            Vh = arena[:, 20480:24576]
            Qh = arena[:, 24576:26624]
            ctxh = arena[:, 26624:28672]
            ctxi = [arena[:, 28672:30720], arena[:, 30720:32768]]

            for h in range(NH):
                rec.pump(13)
                dh = h * 128
                rec.dma("dsp", lambda e, dh=dh: e.dma_start(
                    out=Qh[:, :], in_=q_dram[dh:dh + 128, :]),
                    reads=["qdram"], writes=["Qh"])
                for r in range(2):
                    rec.dma("dsp", lambda e, r=r, dh=dh: e.dma_start(
                        out=Kh[:, r * 2048:(r + 1) * 2048],
                        in_=cc_out.ap()[r, 0, dh:dh + 128, :]),
                        reads=["ccout"], writes=["Kh"])
                    rec.dma("dsp", lambda e, r=r, dh=dh: e.dma_start(
                        out=Vh[:, r * 2048:(r + 1) * 2048].rearrange(
                            "p (m d) -> p m d", d=128),
                        in_=cc_out.ap()[r, 1, :, dh:dh + 128].rearrange(
                            "(m p) d -> p m d", p=128)),
                        reads=["ccout"], writes=["Vh"])
                for i in range(NQT):
                    nr = i + 1
                    ncols = 2 * nr * 128
                    nchunk = 0
                    for r in range(2):
                        wt = nr * 128
                        sb0 = r * wt
                        me = wt - 128
                        c0 = 0
                        while c0 < wt:
                            w = min(512, wt - c0)
                            bn = f"ps{nchunk % 2}"
                            mm(bn, psum[bn][:, 0:w], Qh[:, i * 128:(i + 1) * 128],
                               Kh[:, r * 2048 + c0:r * 2048 + c0 + w],
                               True, True, ["Qh", "Kh"])
                            if c0 + w <= me:
                                rec.op("act", lambda s, bn=bn, d0=sb0 + c0, w=w: s.copy(
                                    out=S_sb[:, d0:d0 + w], in_=psum[bn][:, 0:w]),
                                    reads=[bn], writes=["Ssb"])
                            else:
                                pw = me - c0
                                if pw > 0:
                                    rec.op("act", lambda s, bn=bn, d0=sb0 + c0, pw=pw:
                                           s.copy(out=S_sb[:, d0:d0 + pw],
                                                  in_=psum[bn][:, 0:pw]),
                                           reads=[bn], writes=["Ssb"])
                                rec.op("dve", lambda v, bn=bn, d0=sb0 + me, pw=max(pw, 0),
                                       r=r: v.tensor_add(
                                           S_sb[:, d0:d0 + 128],
                                           psum[bn][:, pw:pw + 128],
                                           mask2[:, r * 128:(r + 1) * 128]),
                                       reads=[bn, "mask2"], writes=["Ssb"])
                            c0 += w
                            nchunk += 1
                    rec.op("dve", lambda v, n=ncols: v.reduce_max(
                        nmx[:, 0:1], S_sb[:, 0:n], axis=AX.X, negate=True),
                        reads=["Ssb"], writes=["nmx"])
                    rec.op("dve", lambda v: v.tensor_scalar_mul(
                        nmx[:, 1:2], nmx[:, 0:1], SCALE), reads=["nmx"], writes=["nmx"])
                    rec.op("act", lambda s, n=ncols: s.activation(
                        out=P_sb[:, 0:n], in_=S_sb[:, 0:n], func=AF.Exp,
                        bias=nmx[:, 1:2], scale=consts[:, 1:2],
                        accum_out=nmx[:, 2:3]),
                        reads=["Ssb", "nmx", "consts"], writes=["Psb", "nmx"])
                    rec.op("dve", lambda v: v.reciprocal(nmx[:, 3:4], nmx[:, 2:3]),
                           reads=["nmx"], writes=["nmx"])
                    rec.op("dve", lambda v, n=ncols: v.tensor_scalar_mul(
                        P_sb[:, 0:n], P_sb[:, 0:n], nmx[:, 3:4]),
                        reads=["nmx", "Psb"], writes=["Psb"])
                    nt = ncols // 128
                    for g in range((nt + 7) // 8):
                        t0, t1 = g * 8, min(g * 8 + 8, nt)
                        bn = f"pstr{g % 2}"
                        for t in range(t0, t1):
                            tr(bn, psum[bn][:, (t - t0) * 128:(t - t0 + 1) * 128],
                               P_sb[:, t * 128:(t + 1) * 128], ["Psb"])
                        rec.op("act", lambda s, bn=bn, t0=t0, w=(t1 - t0) * 128:
                               s.copy(out=PT_sb[:, t0 * 128:t0 * 128 + w],
                                      in_=psum[bn][:, 0:w]),
                               reads=[bn], writes=["PTsb"])
                    cn = f"psc{i % 2}"
                    for t in range(nt):
                        r, m = (0, t) if t < nr else (1, t - nr)
                        vcol = r * 16 + m
                        mm(cn, psum[cn][:, 0:128], Vh[:, vcol * 128:(vcol + 1) * 128],
                           PT_sb[:, t * 128:(t + 1) * 128], t == 0, t == nt - 1,
                           ["Vh", "PTsb"])
                    rec.op("dve", lambda v, cn=cn, i=i: v.tensor_copy(
                        ctxh[:, i * 128:(i + 1) * 128], psum[cn][:, 0:128]),
                        reads=[cn], writes=["ctxh"])
                rec.dma("dsp", lambda e, dh=dh: e.dma_start(
                    out=ctx_dram[dh:dh + 128, :], in_=ctxh[:, :]),
                    reads=["ctxh"], writes=["ctxd"])

            fill_brow(ab["aWOb"] if kind == "a" else bbo)
            for fs in range(4):
                load_set_slab(wbuf[:, 0:8192], WB_ALL, scr_o, fs * 512, 512)
                for i in range(NQT):
                    sl = i % 2
                    rec.dma("dsp", lambda e, i=i, sl=sl: e.dma_start(
                        out=ctxi[sl].rearrange("p (k c) -> p k c", c=128),
                        in_=ctx_dram[:, i * 128:(i + 1) * 128].rearrange(
                            "(k p) c -> p k c", p=128)),
                        reads=["ctxd"], writes=[f"ctxi{sl}"])
                    bn = f"ps{i % 2}"
                    for k in range(16):
                        mm(bn, psum[bn][:, :], ctxi[sl][:, k * 128:(k + 1) * 128],
                           wbuf[:, k * 512:(k + 1) * 512], k == 0, False,
                           [f"ctxi{sl}"] + WB_ALL + [ko])
                    mm(bn, psum[bn][:, :], ones_bf,
                       brow_bf[:, fs * 512:(fs + 1) * 512], False, True,
                       ["onesb", "browbf"])
                    if last:
                        rec.op("dve", lambda v, bn=bn, i=i, fs=fs: v.tensor_add(
                            outst[:, fs * 512:(fs + 1) * 512], psum[bn][:, :],
                            x_res[:, i * C + fs * 512:i * C + fs * 512 + 512]),
                            reads=[bn, f"xres{i}"],
                            writes=["outst", "kvstg0", "kvstg1", "browf"])
                        rec.dma("dsp", lambda e, i=i, fs=fs: e.dma_start(
                            out=out_d[i][:, fs * 512:(fs + 1) * 512],
                            in_=outst[:, fs * 512:(fs + 1) * 512]),
                            reads=["outst"], writes=["outd"])
                    else:
                        rec.op("dve", lambda v, bn=bn, i=i, fs=fs: v.tensor_add(
                            x_res[:, i * C + fs * 512:i * C + fs * 512 + 512],
                            psum[bn][:, :],
                            x_res[:, i * C + fs * 512:i * C + fs * 512 + 512]),
                            reads=[bn, f"xres{i}"], writes=[f"xres{i}"])

        def ffn(last, pump_b):
            rec.barrier()
            ln_phase()

            for i in range(NQT):
                bn = f"ps{i % 2}"
                for k in range(16):
                    mm(bn, psum[bn][:, 0:8],
                       arena[:, k * C + i * 128:k * C + i * 128 + 128],
                       gwT[:, k * 8:(k + 1) * 8], k == 0, k == 15, ["gwT", "xnf"])
                rec.op("act", lambda s, bn=bn, i=i: s.activation(
                    out=sall[:, i * 8:(i + 1) * 8], in_=psum[bn][:, 0:8],
                    func=AF.Sigmoid),
                    reads=[bn], writes=["sall"])
            rec.op("dve", lambda v: v.tensor_add(rt[:, 0:128], sall[:, :], gb_bc[:, :]),
                   reads=["sall", "gbbc"], writes=["rt"])
            rec.op("dve", lambda v: v.reduce_max(
                seg[:, 0:16], rt[:, 0:128].rearrange("p (s e) -> p s e", e=8),
                axis=AX.X), reads=["rt"], writes=["seg"])
            for i in range(NQT):
                rec.op("dve", lambda v, i=i: v.tensor_scalar(
                    rt[:, 256 + i * 8:256 + (i + 1) * 8], rt[:, i * 8:(i + 1) * 8],
                    seg[:, i:i + 1], None, op0=ALU.is_equal),
                    reads=["rt", "seg"], writes=["rt"])
            rec.op("dve", lambda v: v.scalar_tensor_tensor(
                rt[:, 128:256], rt[:, 256:384], NEG, rt[:, 0:128],
                op0=ALU.mult, op1=ALU.add), reads=["rt"], writes=["rt"])
            rec.op("dve", lambda v: v.reduce_max(
                seg[:, 16:32], rt[:, 128:256].rearrange("p (s e) -> p s e", e=8),
                axis=AX.X), reads=["rt"], writes=["seg"])
            for i in range(NQT):
                rec.op("dve", lambda v, i=i: v.tensor_scalar(
                    rt[:, i * 8:(i + 1) * 8], rt[:, 128 + i * 8:128 + (i + 1) * 8],
                    seg[:, 16 + i:17 + i], None, op0=ALU.is_equal),
                    reads=["rt", "seg"], writes=["rt"])
            rec.op("dve", lambda v: v.tensor_add(
                rt[:, 256:384], rt[:, 256:384], rt[:, 0:128]),
                reads=["rt"], writes=["rt"])
            rec.op("dve", lambda v: v.tensor_mul(
                rt[:, 0:128], rt[:, 256:384], sall[:, :]),
                reads=["rt", "sall"], writes=["rt"])
            rec.op("dve", lambda v: v.reduce_sum(
                seg[:, 32:48], rt[:, 0:128].rearrange("p (s e) -> p s e", e=8),
                axis=AX.X), reads=["rt"], writes=["seg"])
            rec.op("dve", lambda v: v.reciprocal(seg[:, 48:64], seg[:, 32:48]),
                   reads=["seg"], writes=["seg"])
            for i in range(NQT):
                rec.op("dve", lambda v, i=i: v.tensor_scalar_mul(
                    weall[:, i * 8:(i + 1) * 8], rt[:, i * 8:(i + 1) * 8],
                    seg[:, 48 + i:49 + i]), reads=["rt", "seg"], writes=["weall"])

            pes = [
                (scr["sgw"], "scr_sgw", [22 + m for m in range(11)], list(range(11)),
                 scr["sdw"], "scr_sdw", list(range(11)), "init", None),
                (scr["sgw"], "scr_sgw", [33 + m for m in range(11)],
                 [11 + m for m in range(11)],
                 scr["sdw"], "scr_sdw", [11 + m for m in range(11)], "add", None),
            ]
            for e in range(NE):
                pes.append((scr[f"up{e}"], f"scr_up{e}", list(range(11)),
                            [11 + m for m in range(11)],
                            scr[f"dn{e}"], f"scr_dn{e}", list(range(11)), "we", e))

            upslot = [wbuf[:, 0:2048], wbuf[:, 2048:4096]]
            dnslot = [wbuf[:, 4096:4608], wbuf[:, 4608:5120]]
            ustg = [wbuf[:, 5120:5632], wbuf[:, 5632:6144]]

            nslot = 0
            for tc in range(4):
                for ups, upkey, gch, lch, dns, dnkey, dnch, mode, eidx in pes:
                    if pump_b:
                        rec.pump(2)
                    for m in range(11):
                        for half in range(2):
                            hc = gch[m] if half == 0 else lch[m]
                            sl = nslot % 2
                            load_lhsT_set(upslot[sl], [f"wb{'AB'[sl]}"], ups, hc)
                            bn = f"ps{nslot % 2}"
                            for k in range(16):
                                mm(bn, psum[bn][:, :],
                                   upslot[sl][:, k * 128:(k + 1) * 128],
                                   arena[:, k * C + tc * 512:k * C + tc * 512 + 512],
                                   k == 0, k == 15, [f"wb{'AB'[sl]}", "xnf", upkey])
                            if half == 0:
                                rec.op("act", lambda s, bn=bn, m=m: s.activation(
                                    out=h_sb[:, m * 512:(m + 1) * 512],
                                    in_=psum[bn][:, :], func=AF.Silu),
                                    reads=[bn],
                                    writes=[f"h{m}"] + (["stgb0"] if m <= 5 else [])
                                    + (["stgb1"] if m >= 5 else []))
                            else:
                                usl = nslot % 2
                                rec.op("act", lambda s, bn=bn, usl=usl: s.copy(
                                    out=ustg[usl][:, :], in_=psum[bn][:, :]),
                                    reads=[bn], writes=[f"ustg{usl}", "xnt"])
                                rec.op("dve", lambda v, m=m, usl=usl: v.tensor_mul(
                                    h_sb[:, m * 512:(m + 1) * 512],
                                    h_sb[:, m * 512:(m + 1) * 512], ustg[usl][:, :]),
                                    reads=[f"ustg{usl}", f"h{m}"],
                                    writes=[f"h{m}"] + (["stgb0"] if m <= 5 else [])
                                    + (["stgb1"] if m >= 5 else []))
                            nslot += 1
                    dbanks = ["ps0", "ps1", "psc0", "psc1"]
                    for fs in range(4):
                        for m in range(11):
                            sl = m % 2
                            rec.dma("dsp", lambda e, sl=sl, dns=dns, m=m, fs=fs,
                                    dnch=dnch: e.dma_start(
                                        out=dnslot[sl][:, :],
                                        in_=dns[dnch[m] * 128:(dnch[m] + 1) * 128,
                                                fs * 512:(fs + 1) * 512]),
                                    reads=[dnkey], writes=[f"dn{sl}", "xnt"])
                            for q in range(4):
                                bn = dbanks[q]
                                mm(bn, psum[bn][:, :],
                                   h_sb[:, m * 512 + q * 128:m * 512 + q * 128 + 128],
                                   dnslot[sl][:, :], m == 0, m == 10,
                                   [f"dn{sl}", f"h{m}"])
                        for q in range(4):
                            bn = dbanks[q]
                            i = tc * 4 + q
                            a_sl = acc[:, q * C + fs * 512:q * C + fs * 512 + 512]
                            akeys = ["acc", "stgf0", "stgf1", "stgb0", "stgb1"]
                            if mode == "init":
                                rec.op("dve", lambda v, bn=bn, a=a_sl: v.tensor_copy(
                                    a, psum[bn][:, :]), reads=[bn], writes=akeys)
                            elif mode == "add":
                                rec.op("dve", lambda v, bn=bn, a=a_sl: v.tensor_add(
                                    a, a, psum[bn][:, :]),
                                    reads=[bn] + ACC_ALL, writes=akeys)
                            else:
                                rec.op("dve", lambda v, bn=bn, a=a_sl, i=i, e=eidx:
                                       v.scalar_tensor_tensor(
                                           a, psum[bn][:, :],
                                           weall[:, i * 8 + e:i * 8 + e + 1], a,
                                           op0=ALU.mult, op1=ALU.add),
                                       reads=[bn, "weall"] + ACC_ALL, writes=akeys)
                for q in range(4):
                    i = tc * 4 + q
                    if last:
                        rec.op("dve", lambda v, i=i, q=q: v.tensor_add(
                            outst[:, :], acc[:, q * C:(q + 1) * C],
                            x_res[:, i * C:(i + 1) * C]),
                            reads=ACC_ALL + [f"xres{i}"],
                            writes=["outst", "kvstg0", "kvstg1", "browf"])
                        rec.dma("dsp", lambda e, i=i: e.dma_start(
                            out=out_d[i], in_=outst[:, :]),
                            reads=["outst"], writes=["outd"])
                    else:
                        rec.op("dve", lambda v, i=i, q=q: v.tensor_add(
                            x_res[:, i * C:(i + 1) * C], acc[:, q * C:(q + 1) * C],
                            x_res[:, i * C:(i + 1) * C]),
                            reads=ACC_ALL + [f"xres{i}"], writes=[f"xres{i}"])

        subs = [("attn", "a"), ("ffn", 1), ("attn", "b"), ("ffn", 2)][:n_sub]
        for si, (kind, arg) in enumerate(subs):
            is_last = si == len(subs) - 1
            if kind == "attn":
                if arg == "b":
                    rec.pump_all()
                attn(arg, is_last)
            else:
                rec.pump_all()
                if arg == 1 and n_sub >= 3:
                    rec.prep_q.extend(attn_b_items())
                ffn(is_last, PUMP_B and arg == 1 and n_sub >= 3)
        rec.pump_all()
        rec.wait_only("sp", {"dsp": rec.dcnt["dsp"]})
        rec.wait_only("pool", {"dsp": rec.dcnt["dsp"], "dact": rec.dcnt["dact"]})

        waited = {}
        for ename in ENGS:
            for waits, fn, sem, val, dbg in rec.ops[ename]:
                for s, v in waits:
                    waited.setdefault(s, set()).add(v)
        remap = {}
        for s, vals in waited.items():
            if s in QUEUES or s == "cc":
                remap[s] = {v: v for v in vals}
            else:
                remap[s] = {v: i + 1 for i, v in enumerate(sorted(vals))}

        with nc.Block() as block, contextlib.ExitStack() as semctx:
            sems = {}
            for s in ("pe", "act", "dve", "pool", "sp", "dsp", "dact", "dpool", "cc"):
                sems[s] = semctx.enter_context(nc.semaphore("sem_" + s))

            def replay(ename, eng):
                for idx, (waits, fn, sem, val, dbg) in enumerate(rec.ops[ename]):
                    for s, v in waits:
                        eng.wait_ge(sems[s], remap[s][v])
                    if fn is not None:
                        try:
                            inst = fn(eng)
                        except Exception as ex:
                            raise RuntimeError(
                                f"replay fail {ename} op#{idx} dbg={dbg}: {ex}") from ex
                        if sem in QUEUES or sem == "cc":
                            inst.then_inc(sems[sem], 16 if sem in QUEUES else 1)
                        elif sem in remap and val in remap[sem]:
                            inst.then_inc(sems[sem], 1)

            @block.tensor
            def _(t):
                replay("pe", t)

            @block.scalar
            def _(s):
                replay("act", s)

            @block.vector
            def _(v):
                replay("dve", v)

            @block.gpsimd
            def _(g):
                replay("pool", g)

            @block.sync
            def _(sy):
                replay("sp", sy)

    return nc


def make_inputs(core, inputs):
    b, h = core // 2, core % 2
    x = np.ascontiguousarray(np.asarray(inputs["L0_attn_a_x"])[b].reshape(32, 128, C)[h::2])
    tri = np.where(np.arange(128)[None, :] <= np.arange(128)[:, None], 0.0,
                   NEG).astype(np.float32)
    if h == 0:
        mask = np.concatenate([tri, np.full((128, 128), NEG, np.float32)], axis=1)
    else:
        mask = np.concatenate([np.zeros((128, 128), np.float32), tri], axis=1)
    d = {
        "x": x,
        "aWQ": inputs["L0_attn_a_WQ_w"], "aWQb": inputs["L0_attn_a_WQ_b"],
        "aWK": inputs["L0_attn_a_WK_w"], "aWKb": inputs["L0_attn_a_WK_b"],
        "aWV": inputs["L0_attn_a_WV_w"], "aWVb": inputs["L0_attn_a_WV_b"],
        "aWO": inputs["L0_attn_a_WO_w"], "aWOb": inputs["L0_attn_a_WO_b"],
        "bwq": inputs["L0_attn_b_wq"], "bwk": inputs["L0_attn_b_wk"],
        "bwv": inputs["L0_attn_b_wv"], "bwo": inputs["L0_attn_b_wo"],
        "bbo": inputs["L0_attn_b_bo"],
        "gw": inputs["L0_ffn_gate_w"],
        "gb_row": np.tile(np.asarray(inputs["L0_ffn_gate_bias"], np.float32),
                          16)[None, :],
        "sgw": inputs["L0_ffn_shared_gw"], "sdw": inputs["L0_ffn_shared_dw"],
        "upw": inputs["L0_ffn_up_w"], "dnw": inputs["L0_ffn_down_w"],
        "idn": np.eye(128, dtype=ml_dtypes.bfloat16),
        "ones_bf": np.ones((1, 128), dtype=ml_dtypes.bfloat16),
        "ones_f": np.ones((1, 128), dtype=np.float32),
        "mask2": mask,
    }
    return {k: np.ascontiguousarray(np.asarray(v)) for k, v in d.items()}


def kernel(**inputs):
    nc = build_nc(4)
    in_maps = [make_inputs(c, inputs) for c in range(8)]
    res = run_bass_kernel_spmd(nc, in_maps, core_ids=list(range(8)))
    out = np.empty((4, 4096, C), dtype=np.float32)
    for c in range(8):
        b, h = c // 2, c % 2
        out[b].reshape(32, 128, C)[h::2] = res.results[c]["out"]
    return out


# version 05
# speedup vs baseline: 1.0186x; 1.0186x over previous
import contextlib
from collections import deque

import numpy as np
import ml_dtypes
import concourse.bass as bass
import concourse.mybir as mybir
from concourse.bass_utils import run_bass_kernel_spmd

F32 = mybir.dt.float32
BF16 = mybir.dt.bfloat16
AF = mybir.ActivationFunctionType
ALU = mybir.AluOpType
AX = mybir.AxisListType

C = 2048
NQT = 16
NH = 16
HS = 2816
HE = 1408
NE = 8
SCALE = 1.0 / float(np.sqrt(128.0))
NEG = -1.0e30
EPS = 1e-5

PUMP_B = True
ENGS = ("pe", "act", "dve", "pool", "sp")
QUEUES = {"dsp": "sp", "dact": "act", "dpool": "pool"}

WB_ALL = ["wbA", "wbB", "wbSS", "dn0", "dn1", "ustg0", "ustg1"]


class Rec:

    def __init__(self):
        self.ops = {e: [] for e in ENGS}
        self.cnt = {e: 0 for e in ENGS}
        self.dcnt = {q: 0 for q in QUEUES}
        self.ccnt = 0
        self.know = {e: {} for e in ENGS}
        self.reg = {}
        self.prep_q = deque()

    def _region(self, r):
        if r not in self.reg:
            self.reg[r] = {"w": {}, "r": {}}
        return self.reg[r]

    def _emit(self, eng, fn, reads, writes, sem, val, skip):
        waits = {}
        for r in reads:
            for s, v in self._region(r)["w"].items():
                if s not in skip and self.know[eng].get(s, 0) < v:
                    waits[s] = max(waits.get(s, 0), v)
        for r in writes:
            rr = self._region(r)
            for s, v in rr["w"].items():
                if s not in skip and self.know[eng].get(s, 0) < v:
                    waits[s] = max(waits.get(s, 0), v)
            for s, v in rr["r"].items():
                if s not in skip and self.know[eng].get(s, 0) < v:
                    waits[s] = max(waits.get(s, 0), v)
        for s, v in waits.items():
            self.know[eng][s] = v
        self.ops[eng].append((sorted(waits.items()), fn, sem, val, (list(reads), list(writes))))
        for r in writes:
            rr = self._region(r)
            rr["w"] = {sem: val}
            rr["r"] = {}
        for r in reads:
            self._region(r)["r"][sem] = val

    def op(self, eng, fn, reads=(), writes=()):
        self.cnt[eng] += 1
        self._emit(eng, fn, reads, writes, eng, self.cnt[eng],
                   (eng,) if eng == "pe" else ())

    def dma(self, q, fn, reads=(), writes=()):
        self.dcnt[q] += 16
        self._emit(QUEUES[q], fn, reads, writes, q, self.dcnt[q], ())

    def coll(self, fn, reads=(), writes=()):
        self.ccnt += 1
        self._emit("pool", fn, reads, writes, "cc", self.ccnt, ())

    def wait_only(self, eng, sems):
        waits = []
        for s, v in sems.items():
            if v > 0 and self.know[eng].get(s, 0) < v:
                waits.append((s, v))
                self.know[eng][s] = v
        if waits:
            self.ops[eng].append((waits, None, None, 0, ([], [])))

    def barrier(self):
        engines = ("pe", "act", "dve", "sp")
        cur = {e: self.cnt[e] for e in engines}
        cur["dsp"] = self.dcnt["dsp"]
        for e in engines:
            self.wait_only(e, {s: v for s, v in cur.items() if s != e})

    def pump(self, n):
        for _ in range(n):
            if not self.prep_q:
                return
            self.prep_q.popleft()()

    def pump_all(self):
        while self.prep_q:
            self.prep_q.popleft()()


def build_nc(n_sub=4):
    nc = bass.Bass("TRN2", target_bir_lowering=False)
    rec = Rec()

    x_in = nc.dram_tensor("x", [NQT, 128, C], F32, kind="ExternalInput").ap()
    out_d = nc.dram_tensor("out", [NQT, 128, C], F32, kind="ExternalOutput").ap()
    aW = {nm: nc.dram_tensor(nm, [C, C], F32, kind="ExternalInput").ap()
          for nm in ("aWQ", "aWK", "aWV", "aWO")}
    ab = {nm: nc.dram_tensor(nm, [C], F32, kind="ExternalInput").ap()
          for nm in ("aWQb", "aWKb", "aWVb", "aWOb")}
    bW = {nm: nc.dram_tensor(nm, [C, C], F32, kind="ExternalInput").ap()
          for nm in ("bwq", "bwk", "bwv", "bwo")}
    bbo = nc.dram_tensor("bbo", [C], F32, kind="ExternalInput").ap()
    gw_in = nc.dram_tensor("gw", [NE, C], F32, kind="ExternalInput").ap()
    gb_row = nc.dram_tensor("gb_row", [1, 128], F32, kind="ExternalInput").ap()
    sgw_in = nc.dram_tensor("sgw", [2 * HS, C], F32, kind="ExternalInput").ap()
    sdw_in = nc.dram_tensor("sdw", [C, HS], F32, kind="ExternalInput").ap()
    upw_in = nc.dram_tensor("upw", [NE, 2 * HE, C], F32, kind="ExternalInput").ap()
    dnw_in = nc.dram_tensor("dnw", [NE, C, HE], F32, kind="ExternalInput").ap()
    idn_in = nc.dram_tensor("idn", [128, 128], BF16, kind="ExternalInput").ap()
    onesb_in = nc.dram_tensor("ones_bf", [1, 128], BF16, kind="ExternalInput").ap()
    onesf_in = nc.dram_tensor("ones_f", [1, 128], F32, kind="ExternalInput").ap()
    mask_in = nc.dram_tensor("mask2", [128, 256], F32, kind="ExternalInput").ap()

    scr = {}
    for nm in ("aWQ", "aWK"):
        scr[nm] = nc.dram_tensor("s_" + nm, [16, 128, C], BF16).ap()
    for nm in ("aWV", "aWO", "bwq", "bwk", "bwv", "bwo"):
        scr[nm] = nc.dram_tensor("s_" + nm, [C, C], BF16).ap()
    scr["sgw"] = nc.dram_tensor("s_sgw", [2 * HS // 128, 128, C], BF16).ap()
    scr["sdw"] = nc.dram_tensor("s_sdw", [HS, C], BF16).ap()
    for e in range(NE):
        scr[f"up{e}"] = nc.dram_tensor(f"s_up{e}", [2 * HE // 128, 128, C], BF16).ap()
        scr[f"dn{e}"] = nc.dram_tensor(f"s_dn{e}", [HE, C], BF16).ap()
    cc_in = nc.dram_tensor("cc_in", [2, C, C], BF16)
    cc_out = nc.dram_tensor("cc_out", [2, 2, C, C], BF16)
    q_dram = nc.dram_tensor("q_dram", [C, C], BF16).ap()
    ctx_dram = nc.dram_tensor("ctx_dram", [C, C], BF16).ap()

    ctx = contextlib.ExitStack()
    sb = lambda name, shape, dt: ctx.enter_context(nc.sbuf_tensor(name, shape, dt))

    with ctx:
        x_res = sb("x_res", [128, NQT * C], BF16)
        arena = sb("arena", [128, NQT * C], BF16)
        ar32 = arena[:, :].bitcast(F32)

        acc = sb("acc", [128, 4 * C], F32)
        h_sb = sb("h_sb", [128, 11 * 512], BF16)
        wbuf = sb("wbuf", [128, 8192], BF16)
        outst = sb("outst", [128, C], F32)
        xnt = wbuf[:, 4096:6144]
        gwb = wbuf[0:8, 0:C]
        outst_bf = outst[:, :].bitcast(BF16)
        idn = sb("sb_idn", [128, 128], BF16)
        mask2 = sb("sb_mask2", [128, 256], F32)
        stats = sb("stats", [128, 96], F32)
        nmx = sb("nmx", [128, 4], F32)
        sall = sb("sall", [128, 128], F32)
        weall = sb("weall", [128, 128], F32)
        rt = sb("rt", [128, 384], F32)
        seg = sb("seg", [128, 64], F32)
        bqc = sb("bqc", [128, 16], F32)
        bkc = sb("bkc", [128, 16], F32)
        brows = sb("brows", [128, C], BF16)
        brow_f = outst[0:1, 0:C]
        brow_bf = brows[0:1, :]
        ones_bf = sb("sb_onesb", [128, 128], BF16)[0:1, :]
        ones_f = sb("sb_onesf", [128, 128], F32)[0:1, :]
        gb_bc = sb("gb_bc", [128, 128], F32)
        consts = sb("consts", [128, 4], F32)
        gwT = sb("gwT", [128, 128], BF16)

        psum = {}
        for nm in ("ps0", "ps1", "psc0", "psc1"):
            psum[nm] = ctx.enter_context(nc.psum_tensor(nm, [128, 512], F32))
        for nm in ("pstr0", "pstr1", "psp0", "psp1"):
            psum[nm] = ctx.enter_context(nc.psum_tensor(nm, [128, 1024], BF16))

        acc_bf = acc[:, :].bitcast(BF16)
        stgf = [acc[:, 0:2048], acc[:, 2048:4096]]
        stgb = [acc_bf[:, 8192:10240], acc_bf[:, 10240:12288]]
        sstgf = outst[:, :]
        sstgb = brows[:, :]

        ACC_ALL = ["acc", "stgf0", "stgf1", "stgb0", "stgb1"]

        def mm(bname, out_ap, lhsT, rhs, start, stop, reads):
            rec.op("pe", lambda t, o=out_ap, l=lhsT, r=rhs, s=start, p=stop:
                   t.matmul(o, l, r, start=s, stop=p),
                   reads=list(reads), writes=[bname])

        def tr(bname, out_ap, in_ap, reads, np_=128):
            rec.op("pe", lambda t, o=out_ap, i=in_ap, n=np_:
                   t.transpose(o, i, idn[0:n, 0:n]),
                   reads=list(reads) + ["idn"], writes=[bname])

        rec.op("dve", lambda v: v.memset(consts[:, 1:2], SCALE), writes=["consts"])
        rec.dma("dsp", lambda e: e.dma_start(out=idn[:, :], in_=idn_in), writes=["idn"])
        rec.dma("dsp", lambda e: e.dma_start(out=mask2[:, :], in_=mask_in), writes=["mask2"])
        rec.dma("dsp", lambda e: e.dma_start(out=ones_bf, in_=onesb_in), writes=["onesb"])
        rec.dma("dsp", lambda e: e.dma_start(out=ones_f, in_=onesf_in), writes=["onesf"])
        rec.dma("dsp", lambda e: e.dma_start(
            out=bqc[:, :], in_=ab["aWQb"].rearrange("(k p) -> p k", p=128),
            allow_slow_non_contiguous=True), writes=["bqc"])
        rec.dma("dsp", lambda e: e.dma_start(
            out=bkc[:, :], in_=ab["aWKb"].rearrange("(k p) -> p k", p=128),
            allow_slow_non_contiguous=True), writes=["bkc"])
        def fill_brow(src_ap):
            rec.dma("dsp", lambda e, s=src_ap: e.dma_start(
                out=brow_f, in_=s.rearrange("(o c) -> o c", o=1)),
                writes=["browf", "outst", "kvstg0", "kvstg1"])
            rec.op("dve", lambda v: v.tensor_copy(brow_bf, brow_f),
                   reads=["browf"], writes=["browbf"])
        rec.dma("dsp", lambda e: e.dma_start(out=brow_f[:, 0:128], in_=gb_row),
                writes=["browf", "outst", "kvstg0", "kvstg1"])
        rec.op("pe", lambda t: t.matmul(psum["ps0"][:, 0:128], ones_f,
                                        brow_f[:, 0:128], start=True, stop=True),
               reads=["onesf", "browf"], writes=["ps0"])
        rec.op("dve", lambda v: v.tensor_copy(gb_bc[:, :], psum["ps0"][:, 0:128]),
               reads=["ps0"], writes=["gbbc"])
        rec.dma("dsp", lambda e: e.dma_start(out=stgf[0][0:8, 0:C], in_=gw_in),
                writes=["stgf0"])
        rec.op("pool", lambda g: g.tensor_copy(gwb[:, :], stgf[0][0:8, 0:C]),
               reads=["stgf0"], writes=["gwb", "wbA"])
        for k in range(16):
            tr("pstr0", psum["pstr0"][:, k * 8:(k + 1) * 8],
               gwb[0:8, k * 128:(k + 1) * 128], ["gwb", "wbA"], np_=8)
        rec.op("act", lambda s: s.copy(out=gwT[:, :], in_=psum["pstr0"][:, 0:128]),
               reads=["pstr0"], writes=["gwT"])

        for i in range(NQT):
            rec.dma("dsp", lambda e, i=i: e.dma_start(out=outst[:, :], in_=x_in[i]),
                    writes=["outst", "kvstg0", "kvstg1", "browf"])
            rec.op("dve", lambda v, i=i: v.tensor_copy(
                x_res[:, i * C:(i + 1) * C], outst[:, :]),
                reads=["outst"], writes=[f"xres{i}"])

        def prep_t3_item(w_ap, scr3_ap, scrkey, fi, slot):
            f_ap, b_ap = stgf[slot], stgb[slot]
            fkeys = [f"stgf{slot}"]
            bkeys = [f"stgb{slot}"]
            rec.dma("dact", lambda e, w=w_ap, fi=fi, f=f_ap: e.dma_start(
                out=f[:, :], in_=w[fi * 128:(fi + 1) * 128, :]), writes=fkeys)
            rec.op("pool", lambda g, f=f_ap, b=b_ap: g.tensor_copy(
                b[:, :], f[:, :]), reads=fkeys, writes=bkeys)
            for g in range(2):
                bn = f"psp{g}"
                for t in range(8):
                    k = g * 8 + t
                    tr(bn, psum[bn][:, t * 128:(t + 1) * 128],
                       b_ap[:, k * 128:(k + 1) * 128], bkeys)
                rec.op("act", lambda s, bn=bn, a=g * 1024, b=b_ap:
                       s.copy(out=b[:, a:a + 1024], in_=psum[bn][:, :]),
                       reads=[bn], writes=bkeys)
            rec.dma("dact", lambda e, sc=scr3_ap, fi=fi, b=b_ap: e.dma_start(
                out=sc[fi], in_=b[:, :]), reads=bkeys, writes=[scrkey])

        def prep_col_item(w_ap, scr_ap, scrkey, F, k, slot):
            f_ap, b_ap = stgf[slot], stgb[slot]
            fkeys = [f"stgf{slot}"]
            bkeys = [f"stgb{slot}"]
            rec.dma("dact", lambda e, w=w_ap, k=k, f=f_ap, n=F: e.dma_start(
                out=f[:, 0:n].rearrange("p (fc c) -> p fc c", c=128),
                in_=w[:, k * 128:(k + 1) * 128].rearrange("(fc p) c -> p fc c", p=128)),
                writes=fkeys)
            rec.op("pool", lambda g, f=f_ap, b=b_ap, n=F: g.tensor_copy(
                b[:, 0:n], f[:, 0:n]), reads=fkeys, writes=bkeys)
            for g in range((F // 128 + 7) // 8):
                t0, t1 = g * 8, min(g * 8 + 8, F // 128)
                bn = f"psp{g % 2}"
                for t in range(t0, t1):
                    tr(bn, psum[bn][:, (t - t0) * 128:(t - t0 + 1) * 128],
                       b_ap[:, t * 128:(t + 1) * 128], bkeys)
                rec.op("act", lambda s, bn=bn, a=t0 * 128, w=(t1 - t0) * 128, b=b_ap:
                       s.copy(out=b[:, a:a + w], in_=psum[bn][:, 0:w]),
                       reads=[bn], writes=bkeys)
            rec.dma("dact", lambda e, sc=scr_ap, k=k, b=b_ap, n=F: e.dma_start(
                out=sc[k * 128:(k + 1) * 128, :], in_=b[:, 0:n]),
                reads=bkeys, writes=[scrkey])

        def prep_natural_item(w_ap, scr_ap, scrkey, F, k):
            rec.dma("dact", lambda e, w=w_ap, k=k: e.dma_start(
                out=sstgf[:, 0:F], in_=w[k * 128:(k + 1) * 128, :]),
                writes=["outst", "kvstg0", "kvstg1", "browf"])
            rec.op("pool", lambda g: g.tensor_copy(
                sstgb[:, 0:F], sstgf[:, 0:F]), reads=["outst"], writes=["browbf"])
            rec.dma("dact", lambda e, sc=scr_ap, k=k: e.dma_start(
                out=sc[k * 128:(k + 1) * 128, :], in_=sstgb[:, 0:F]),
                reads=["browbf"], writes=[scrkey])

        def attn_a_items():
            its = []
            ns = [0]
            for nm in ("aWQ", "aWK"):
                for fi in range(16):
                    its.append(lambda nm=nm, fi=fi, s=ns[0] % 2: prep_t3_item(
                        aW[nm], scr[nm], "scr_" + nm, fi, s))
                    ns[0] += 1
            for nm in ("aWV", "aWO"):
                for k in range(16):
                    its.append(lambda nm=nm, k=k, s=ns[0] % 2: prep_col_item(
                        aW[nm], scr[nm], "scr_" + nm, C, k, s))
                    ns[0] += 1
            return its

        def ffn_items():
            its = []
            ns = [0]
            for fi in range(2 * HS // 128):
                its.append(lambda fi=fi, s=ns[0] % 2: prep_t3_item(
                    sgw_in, scr["sgw"], "scr_sgw", fi, s))
                ns[0] += 1
            for k in range(HS // 128):
                its.append(lambda k=k, s=ns[0] % 2: prep_col_item(
                    sdw_in, scr["sdw"], "scr_sdw", C, k, s))
                ns[0] += 1
            for e in range(NE):
                for fi in range(2 * HE // 128):
                    its.append(lambda e=e, fi=fi, s=ns[0] % 2: prep_t3_item(
                        upw_in[e], scr[f"up{e}"], f"scr_up{e}", fi, s))
                    ns[0] += 1
                for k in range(HE // 128):
                    its.append(lambda e=e, k=k, s=ns[0] % 2: prep_col_item(
                        dnw_in[e], scr[f"dn{e}"], f"scr_dn{e}", C, k, s))
                    ns[0] += 1
            return its

        def attn_b_items():
            its = []
            for nm in ("bwq", "bwk", "bwv", "bwo"):
                for k in range(16):
                    its.append(lambda nm=nm, k=k: prep_natural_item(
                        bW[nm], scr[nm], "scr_" + nm, C, k))
            return its

        rec.prep_q.extend(attn_a_items())
        rec.pump_all()
        if n_sub >= 2:
            rec.prep_q.extend(ffn_items())

        def ln_phase():
            for i in range(NQT):
                xr = x_res[:, i * C:(i + 1) * C]
                rec.op("dve", lambda v, x=xr, i=i: v.reduce_sum(
                    stats[:, i:i + 1], x, axis=AX.X),
                    reads=[f"xres{i}"], writes=["st_sum"])
                rec.op("act", lambda s, x=xr, i=i: s.activation(
                    out=xnt[:, :], in_=x, func=AF.Square,
                    accum_out=stats[:, 16 + i:17 + i]),
                    reads=[f"xres{i}"], writes=["xnt", "dn0", "dn1", "ustg0", "ustg1", "st_sq"])
            rec.op("dve", lambda v: v.tensor_scalar_mul(
                stats[:, 32:48], stats[:, 0:16], 1.0 / C),
                reads=["st_sum"], writes=["st_mean"])
            rec.op("dve", lambda v: v.tensor_scalar_mul(
                stats[:, 48:64], stats[:, 16:32], 1.0 / C),
                reads=["st_sq"], writes=["st_var"])
            rec.op("dve", lambda v: v.tensor_mul(
                stats[:, 0:16], stats[:, 32:48], stats[:, 32:48]),
                reads=["st_mean"], writes=["st_sum"])
            rec.op("dve", lambda v: v.tensor_sub(
                stats[:, 48:64], stats[:, 48:64], stats[:, 0:16]),
                reads=["st_sum", "st_var"], writes=["st_var"])
            rec.op("dve", lambda v: v.tensor_scalar_add(
                stats[:, 48:64], stats[:, 48:64], EPS),
                reads=["st_var"], writes=["st_var"])
            rec.op("act", lambda s: s.activation(
                out=stats[:, 48:64], in_=stats[:, 48:64], func=AF.Ln),
                reads=["st_var"], writes=["st_var"])
            rec.op("dve", lambda v: v.tensor_scalar_mul(
                stats[:, 48:64], stats[:, 48:64], -0.5),
                reads=["st_var"], writes=["st_var"])
            rec.op("act", lambda s: s.activation(
                out=stats[:, 64:80], in_=stats[:, 48:64], func=AF.Exp),
                reads=["st_var"], writes=["st_rstd"])
            rec.op("dve", lambda v: v.scalar_tensor_tensor(
                stats[:, 80:96], stats[:, 32:48], -1.0, stats[:, 64:80],
                op0=ALU.mult, op1=ALU.mult),
                reads=["st_mean", "st_rstd"], writes=["st_nmr"])
            for i in range(NQT):
                xr = x_res[:, i * C:(i + 1) * C]
                rec.op("act", lambda s, x=xr, i=i: s.activation(
                    out=xnt[:, :], in_=x, func=AF.Identity,
                    bias=stats[:, 80 + i:81 + i], scale=stats[:, 64 + i:65 + i]),
                    reads=[f"xres{i}", "st_rstd", "st_nmr"], writes=["xnt", "dn0", "dn1", "ustg0", "ustg1"])
                for g in range(2):
                    bn = f"pstr{g}"
                    for t in range(8):
                        k = g * 8 + t
                        tr(bn, psum[bn][:, t * 128:(t + 1) * 128],
                           xnt[:, k * 128:(k + 1) * 128], ["xnt"])
                    rec.op("act", lambda s, bn=bn, g=g, i=i: s.copy(
                        out=arena[:, :].rearrange("p (k t) -> p k t", k=16)
                        [:, g * 8:(g + 1) * 8, i * 128:(i + 1) * 128],
                        in_=psum[bn][:, :].rearrange("p (k t) -> p k t", k=8)),
                        reads=[bn], writes=["xnf"])

        def load_lhsT_set(dst_ap, dstkeys, scr3_ap, fc):
            rec.dma("dsp", lambda e, d=dst_ap, s=scr3_ap, fc=fc: e.dma_start(
                out=d, in_=s[fc]), writes=list(dstkeys))

        def load_lhsT_set_plain(dst_ap, dstkeys, scr_ap, fc):
            rec.dma("dsp", lambda e, d=dst_ap, s=scr_ap, fc=fc: e.dma_start(
                out=d.rearrange("p (k c) -> p k c", c=128),
                in_=s.rearrange("(k p) f -> p k f", p=128)[:, :, fc * 128:(fc + 1) * 128]),
                writes=list(dstkeys))

        def load_set_slab(dst_ap, dstkeys, scr_ap, f0, w):
            rec.dma("dsp", lambda e, d=dst_ap, s=scr_ap, f0=f0, w=w: e.dma_start(
                out=d.rearrange("p (k c) -> p k c", c=w),
                in_=s.rearrange("(k p) f -> p k f", p=128)[:, :, f0:f0 + w]),
                writes=list(dstkeys))

        def attn(kind, last):
            scr_q = scr["aWQ" if kind == "a" else "bwq"]
            scr_k = scr["aWK" if kind == "a" else "bwk"]
            scr_v = scr["aWV" if kind == "a" else "bwv"]
            scr_o = scr["aWO" if kind == "a" else "bwo"]
            kq = "scr_" + ("aWQ" if kind == "a" else "bwq")
            kk = "scr_" + ("aWK" if kind == "a" else "bwk")
            kv = "scr_" + ("aWV" if kind == "a" else "bwv")
            ko = "scr_" + ("aWO" if kind == "a" else "bwo")

            rec.barrier()
            ln_phase()

            kvstg = [outst_bf[:, 0:512], outst_bf[:, 512:1024]]
            wslot = [wbuf[:, 0:2048], wbuf[:, 2048:4096]]

            for proj, s_ap, s_key, bias, dest, destkey in (
                    ("k", scr_k, kk, bkc if kind == "a" else None, cc_in.ap()[0], "ccin"),
                    ("q", scr_q, kq, bqc if kind == "a" else None, q_dram, "qdram")):
                for dc in range(16):
                    sl = dc % 2
                    if kind == "a":
                        load_lhsT_set(wslot[sl], [f"wb{'AB'[sl]}"], s_ap, dc)
                    else:
                        load_lhsT_set_plain(wslot[sl], [f"wb{'AB'[sl]}"], s_ap, dc)
                    if dc % 4 == 0:
                        rec.pump(2)
                    for tch in range(4):
                        bn = f"ps{(dc * 4 + tch) % 2}"
                        for k in range(16):
                            mm(bn, psum[bn][:, :], wslot[sl][:, k * 128:(k + 1) * 128],
                               arena[:, k * C + tch * 512:k * C + tch * 512 + 512],
                               k == 0, k == 15, [f"wb{'AB'[sl]}", "xnf", s_key])
                        st = (dc * 4 + tch) % 2
                        if bias is not None:
                            rec.op("act", lambda s, bn=bn, st=st, dc=dc, b=bias:
                                   s.activation(out=kvstg[st][:, :], in_=psum[bn][:, :],
                                                func=AF.Identity, bias=b[:, dc:dc + 1]),
                                   reads=[bn, "bqc", "bkc"],
                                   writes=[f"kvstg{st}", "browf"])
                        else:
                            rec.op("act", lambda s, bn=bn, st=st:
                                   s.copy(out=kvstg[st][:, :], in_=psum[bn][:, :]),
                                   reads=[bn], writes=[f"kvstg{st}", "browf"])
                        rec.dma("dsp", lambda e, d=dest, dc=dc, tch=tch, st=st:
                                e.dma_start(out=d[dc * 128:(dc + 1) * 128,
                                                  tch * 512:(tch + 1) * 512],
                                            in_=kvstg[st][:, :]),
                                reads=[f"kvstg{st}"], writes=[destkey])

            if kind == "a":
                fill_brow(ab["aWVb"])
            for ds in range(4):
                load_set_slab(wbuf[:, 0:8192], WB_ALL, scr_v, ds * 512, 512)
                rec.pump(4)
                for tt in range(16):
                    bn = f"ps{(ds * 16 + tt) % 2}"
                    for k in range(16):
                        mm(bn, psum[bn][:, :],
                           arena[:, k * C + tt * 128:k * C + tt * 128 + 128],
                           wbuf[:, k * 512:(k + 1) * 512],
                           k == 0, k == 15 and kind != "a", WB_ALL + ["xnf", kv])
                    if kind == "a":
                        mm(bn, psum[bn][:, :], ones_bf,
                           brow_bf[:, ds * 512:(ds + 1) * 512],
                           False, True, ["onesb", "browbf"])
                    st = (ds * 16 + tt) % 2
                    rec.op("act", lambda s, bn=bn, st=st:
                           s.copy(out=kvstg[st][:, :], in_=psum[bn][:, :]),
                           reads=[bn], writes=[f"kvstg{st}", "browf"])
                    rec.dma("dsp", lambda e, tt=tt, ds=ds, st=st:
                            e.dma_start(out=cc_in.ap()[1][tt * 128:(tt + 1) * 128,
                                                          ds * 512:(ds + 1) * 512],
                                        in_=kvstg[st][:, :]),
                            reads=[f"kvstg{st}"], writes=["ccin"])

            rec.coll(lambda g: g.collective_compute(
                "AllGather", ALU.bypass,
                replica_groups=[[0, 1], [2, 3], [4, 5], [6, 7]],
                ins=[cc_in.ap().opt()], outs=[cc_out.ap().opt()]),
                reads=["ccin"], writes=["ccout"])

            rec.barrier()

            S_sb = ar32[:, 0:4096]
            P_sb = arena[:, 8192:12288]
            PT_sb = arena[:, 12288:16384]
            Kh = arena[:, 16384:20480]
            Vh = arena[:, 20480:24576]
            Qh = arena[:, 24576:26624]
            ctxh = arena[:, 26624:28672]
            ctxi = [arena[:, 28672:30720], arena[:, 30720:32768]]

            for h in range(NH):
                rec.pump(13)
                dh = h * 128
                rec.dma("dsp", lambda e, dh=dh: e.dma_start(
                    out=Qh[:, :], in_=q_dram[dh:dh + 128, :]),
                    reads=["qdram"], writes=["Qh"])
                for r in range(2):
                    rec.dma("dsp", lambda e, r=r, dh=dh: e.dma_start(
                        out=Kh[:, r * 2048:(r + 1) * 2048],
                        in_=cc_out.ap()[r, 0, dh:dh + 128, :]),
                        reads=["ccout"], writes=["Kh"])
                    rec.dma("dsp", lambda e, r=r, dh=dh: e.dma_start(
                        out=Vh[:, r * 2048:(r + 1) * 2048].rearrange(
                            "p (m d) -> p m d", d=128),
                        in_=cc_out.ap()[r, 1, :, dh:dh + 128].rearrange(
                            "(m p) d -> p m d", p=128)),
                        reads=["ccout"], writes=["Vh"])
                for i in range(NQT):
                    nr = i + 1
                    ncols = 2 * nr * 128
                    nchunk = 0
                    for r in range(2):
                        wt = nr * 128
                        sb0 = r * wt
                        me = wt - 128
                        c0 = 0
                        while c0 < wt:
                            w = min(512, wt - c0)
                            bn = f"ps{nchunk % 2}"
                            mm(bn, psum[bn][:, 0:w], Qh[:, i * 128:(i + 1) * 128],
                               Kh[:, r * 2048 + c0:r * 2048 + c0 + w],
                               True, True, ["Qh", "Kh"])
                            if c0 + w <= me:
                                rec.op("act", lambda s, bn=bn, d0=sb0 + c0, w=w: s.copy(
                                    out=S_sb[:, d0:d0 + w], in_=psum[bn][:, 0:w]),
                                    reads=[bn], writes=["Ssb"])
                            else:
                                pw = me - c0
                                if pw > 0:
                                    rec.op("act", lambda s, bn=bn, d0=sb0 + c0, pw=pw:
                                           s.copy(out=S_sb[:, d0:d0 + pw],
                                                  in_=psum[bn][:, 0:pw]),
                                           reads=[bn], writes=["Ssb"])
                                rec.op("dve", lambda v, bn=bn, d0=sb0 + me, pw=max(pw, 0),
                                       r=r: v.tensor_add(
                                           S_sb[:, d0:d0 + 128],
                                           psum[bn][:, pw:pw + 128],
                                           mask2[:, r * 128:(r + 1) * 128]),
                                       reads=[bn, "mask2"], writes=["Ssb"])
                            c0 += w
                            nchunk += 1
                    rec.op("dve", lambda v, n=ncols: v.reduce_max(
                        nmx[:, 0:1], S_sb[:, 0:n], axis=AX.X, negate=True),
                        reads=["Ssb"], writes=["nmx"])
                    rec.op("dve", lambda v: v.tensor_scalar_mul(
                        nmx[:, 1:2], nmx[:, 0:1], SCALE), reads=["nmx"], writes=["nmx"])
                    rec.op("act", lambda s, n=ncols: s.activation(
                        out=P_sb[:, 0:n], in_=S_sb[:, 0:n], func=AF.Exp,
                        bias=nmx[:, 1:2], scale=consts[:, 1:2],
                        accum_out=nmx[:, 2:3]),
                        reads=["Ssb", "nmx", "consts"], writes=["Psb", "nmx"])
                    rec.op("dve", lambda v: v.reciprocal(nmx[:, 3:4], nmx[:, 2:3]),
                           reads=["nmx"], writes=["nmx"])
                    rec.op("dve", lambda v, n=ncols: v.tensor_scalar_mul(
                        P_sb[:, 0:n], P_sb[:, 0:n], nmx[:, 3:4]),
                        reads=["nmx", "Psb"], writes=["Psb"])
                    nt = ncols // 128
                    for g in range((nt + 7) // 8):
                        t0, t1 = g * 8, min(g * 8 + 8, nt)
                        bn = f"pstr{g % 2}"
                        for t in range(t0, t1):
                            tr(bn, psum[bn][:, (t - t0) * 128:(t - t0 + 1) * 128],
                               P_sb[:, t * 128:(t + 1) * 128], ["Psb"])
                        rec.op("act", lambda s, bn=bn, t0=t0, w=(t1 - t0) * 128:
                               s.copy(out=PT_sb[:, t0 * 128:t0 * 128 + w],
                                      in_=psum[bn][:, 0:w]),
                               reads=[bn], writes=["PTsb"])
                    cn = f"psc{i % 2}"
                    for t in range(nt):
                        r, m = (0, t) if t < nr else (1, t - nr)
                        vcol = r * 16 + m
                        mm(cn, psum[cn][:, 0:128], Vh[:, vcol * 128:(vcol + 1) * 128],
                           PT_sb[:, t * 128:(t + 1) * 128], t == 0, t == nt - 1,
                           ["Vh", "PTsb"])
                    rec.op("dve", lambda v, cn=cn, i=i: v.tensor_copy(
                        ctxh[:, i * 128:(i + 1) * 128], psum[cn][:, 0:128]),
                        reads=[cn], writes=["ctxh"])
                rec.dma("dsp", lambda e, dh=dh: e.dma_start(
                    out=ctx_dram[dh:dh + 128, :], in_=ctxh[:, :]),
                    reads=["ctxh"], writes=["ctxd"])

            fill_brow(ab["aWOb"] if kind == "a" else bbo)
            for fs in range(4):
                load_set_slab(wbuf[:, 0:8192], WB_ALL, scr_o, fs * 512, 512)
                for i in range(NQT):
                    sl = i % 2
                    rec.dma("dsp", lambda e, i=i, sl=sl: e.dma_start(
                        out=ctxi[sl].rearrange("p (k c) -> p k c", c=128),
                        in_=ctx_dram[:, i * 128:(i + 1) * 128].rearrange(
                            "(k p) c -> p k c", p=128)),
                        reads=["ctxd"], writes=[f"ctxi{sl}"])
                    bn = f"ps{i % 2}"
                    for k in range(16):
                        mm(bn, psum[bn][:, :], ctxi[sl][:, k * 128:(k + 1) * 128],
                           wbuf[:, k * 512:(k + 1) * 512], k == 0, False,
                           [f"ctxi{sl}"] + WB_ALL + [ko])
                    mm(bn, psum[bn][:, :], ones_bf,
                       brow_bf[:, fs * 512:(fs + 1) * 512], False, True,
                       ["onesb", "browbf"])
                    if last:
                        rec.op("dve", lambda v, bn=bn, i=i, fs=fs: v.tensor_add(
                            outst[:, fs * 512:(fs + 1) * 512], psum[bn][:, :],
                            x_res[:, i * C + fs * 512:i * C + fs * 512 + 512]),
                            reads=[bn, f"xres{i}"],
                            writes=["outst", "kvstg0", "kvstg1", "browf"])
                        rec.dma("dsp", lambda e, i=i, fs=fs: e.dma_start(
                            out=out_d[i][:, fs * 512:(fs + 1) * 512],
                            in_=outst[:, fs * 512:(fs + 1) * 512]),
                            reads=["outst"], writes=["outd"])
                    else:
                        rec.op("dve", lambda v, bn=bn, i=i, fs=fs: v.tensor_add(
                            x_res[:, i * C + fs * 512:i * C + fs * 512 + 512],
                            psum[bn][:, :],
                            x_res[:, i * C + fs * 512:i * C + fs * 512 + 512]),
                            reads=[bn, f"xres{i}"], writes=[f"xres{i}"])

        def ffn(last, pump_b):
            rec.barrier()
            ln_phase()

            for i in range(NQT):
                bn = f"ps{i % 2}"
                for k in range(16):
                    mm(bn, psum[bn][:, 0:8],
                       arena[:, k * C + i * 128:k * C + i * 128 + 128],
                       gwT[:, k * 8:(k + 1) * 8], k == 0, k == 15, ["gwT", "xnf"])
                rec.op("act", lambda s, bn=bn, i=i: s.activation(
                    out=sall[:, i * 8:(i + 1) * 8], in_=psum[bn][:, 0:8],
                    func=AF.Sigmoid),
                    reads=[bn], writes=["sall"])
            rec.op("dve", lambda v: v.tensor_add(rt[:, 0:128], sall[:, :], gb_bc[:, :]),
                   reads=["sall", "gbbc"], writes=["rt"])
            rec.op("dve", lambda v: v.reduce_max(
                seg[:, 0:16], rt[:, 0:128].rearrange("p (s e) -> p s e", e=8),
                axis=AX.X), reads=["rt"], writes=["seg"])
            for i in range(NQT):
                rec.op("dve", lambda v, i=i: v.tensor_scalar(
                    rt[:, 256 + i * 8:256 + (i + 1) * 8], rt[:, i * 8:(i + 1) * 8],
                    seg[:, i:i + 1], None, op0=ALU.is_equal),
                    reads=["rt", "seg"], writes=["rt"])
            rec.op("dve", lambda v: v.scalar_tensor_tensor(
                rt[:, 128:256], rt[:, 256:384], NEG, rt[:, 0:128],
                op0=ALU.mult, op1=ALU.add), reads=["rt"], writes=["rt"])
            rec.op("dve", lambda v: v.reduce_max(
                seg[:, 16:32], rt[:, 128:256].rearrange("p (s e) -> p s e", e=8),
                axis=AX.X), reads=["rt"], writes=["seg"])
            for i in range(NQT):
                rec.op("dve", lambda v, i=i: v.tensor_scalar(
                    rt[:, i * 8:(i + 1) * 8], rt[:, 128 + i * 8:128 + (i + 1) * 8],
                    seg[:, 16 + i:17 + i], None, op0=ALU.is_equal),
                    reads=["rt", "seg"], writes=["rt"])
            rec.op("dve", lambda v: v.tensor_add(
                rt[:, 256:384], rt[:, 256:384], rt[:, 0:128]),
                reads=["rt"], writes=["rt"])
            rec.op("dve", lambda v: v.tensor_mul(
                rt[:, 0:128], rt[:, 256:384], sall[:, :]),
                reads=["rt", "sall"], writes=["rt"])
            rec.op("dve", lambda v: v.reduce_sum(
                seg[:, 32:48], rt[:, 0:128].rearrange("p (s e) -> p s e", e=8),
                axis=AX.X), reads=["rt"], writes=["seg"])
            rec.op("dve", lambda v: v.reciprocal(seg[:, 48:64], seg[:, 32:48]),
                   reads=["seg"], writes=["seg"])
            for i in range(NQT):
                rec.op("dve", lambda v, i=i: v.tensor_scalar_mul(
                    weall[:, i * 8:(i + 1) * 8], rt[:, i * 8:(i + 1) * 8],
                    seg[:, 48 + i:49 + i]), reads=["rt", "seg"], writes=["weall"])

            pes = [
                (scr["sgw"], "scr_sgw", [22 + m for m in range(11)], list(range(11)),
                 scr["sdw"], "scr_sdw", list(range(11)), "init", None),
                (scr["sgw"], "scr_sgw", [33 + m for m in range(11)],
                 [11 + m for m in range(11)],
                 scr["sdw"], "scr_sdw", [11 + m for m in range(11)], "add", None),
            ]
            for e in range(NE):
                pes.append((scr[f"up{e}"], f"scr_up{e}", list(range(11)),
                            [11 + m for m in range(11)],
                            scr[f"dn{e}"], f"scr_dn{e}", list(range(11)), "we", e))

            upslot = [wbuf[:, 0:2048], wbuf[:, 2048:4096]]
            dnslot = [wbuf[:, 4096:4608], wbuf[:, 4608:5120]]
            ustg = [wbuf[:, 5120:5632], wbuf[:, 5632:6144]]

            nslot = 0
            for tc in range(4):
                for ups, upkey, gch, lch, dns, dnkey, dnch, mode, eidx in pes:
                    if pump_b:
                        rec.pump(2)
                    for m in range(11):
                        for half in range(2):
                            hc = gch[m] if half == 0 else lch[m]
                            sl = nslot % 2
                            load_lhsT_set(upslot[sl], [f"wb{'AB'[sl]}"], ups, hc)
                            bn = f"ps{nslot % 2}"
                            for k in range(16):
                                mm(bn, psum[bn][:, :],
                                   upslot[sl][:, k * 128:(k + 1) * 128],
                                   arena[:, k * C + tc * 512:k * C + tc * 512 + 512],
                                   k == 0, k == 15, [f"wb{'AB'[sl]}", "xnf", upkey])
                            if half == 0:
                                rec.op("act", lambda s, bn=bn, m=m: s.activation(
                                    out=h_sb[:, m * 512:(m + 1) * 512],
                                    in_=psum[bn][:, :], func=AF.Silu),
                                    reads=[bn],
                                    writes=[f"h{m}"] + (["stgb0"] if m <= 5 else [])
                                    + (["stgb1"] if m >= 5 else []))
                            else:
                                usl = nslot % 2
                                rec.op("act", lambda s, bn=bn, usl=usl: s.copy(
                                    out=ustg[usl][:, :], in_=psum[bn][:, :]),
                                    reads=[bn], writes=[f"ustg{usl}", "xnt"])
                                rec.op("dve", lambda v, m=m, usl=usl: v.tensor_mul(
                                    h_sb[:, m * 512:(m + 1) * 512],
                                    h_sb[:, m * 512:(m + 1) * 512], ustg[usl][:, :]),
                                    reads=[f"ustg{usl}", f"h{m}"],
                                    writes=[f"h{m}"] + (["stgb0"] if m <= 5 else [])
                                    + (["stgb1"] if m >= 5 else []))
                            nslot += 1
                    dbanks = ["ps0", "ps1", "psc0", "psc1"]
                    for fs in range(4):
                        for m in range(11):
                            sl = m % 2
                            rec.dma("dsp", lambda e, sl=sl, dns=dns, m=m, fs=fs,
                                    dnch=dnch: e.dma_start(
                                        out=dnslot[sl][:, :],
                                        in_=dns[dnch[m] * 128:(dnch[m] + 1) * 128,
                                                fs * 512:(fs + 1) * 512]),
                                    reads=[dnkey], writes=[f"dn{sl}", "xnt"])
                            for q in range(4):
                                bn = dbanks[q]
                                mm(bn, psum[bn][:, :],
                                   h_sb[:, m * 512 + q * 128:m * 512 + q * 128 + 128],
                                   dnslot[sl][:, :], m == 0, m == 10,
                                   [f"dn{sl}", f"h{m}"])
                        for q in range(4):
                            bn = dbanks[q]
                            i = tc * 4 + q
                            a_sl = acc[:, q * C + fs * 512:q * C + fs * 512 + 512]
                            akeys = ["acc", "stgf0", "stgf1", "stgb0", "stgb1"]
                            if mode == "init":
                                rec.op("dve", lambda v, bn=bn, a=a_sl: v.tensor_copy(
                                    a, psum[bn][:, :]), reads=[bn], writes=akeys)
                            elif mode == "add":
                                rec.op("dve", lambda v, bn=bn, a=a_sl: v.tensor_add(
                                    a, a, psum[bn][:, :]),
                                    reads=[bn] + ACC_ALL, writes=akeys)
                            else:
                                rec.op("dve", lambda v, bn=bn, a=a_sl, i=i, e=eidx:
                                       v.scalar_tensor_tensor(
                                           a, psum[bn][:, :],
                                           weall[:, i * 8 + e:i * 8 + e + 1], a,
                                           op0=ALU.mult, op1=ALU.add),
                                       reads=[bn, "weall"] + ACC_ALL, writes=akeys)
                for q in range(4):
                    i = tc * 4 + q
                    if last:
                        rec.op("dve", lambda v, i=i, q=q: v.tensor_add(
                            outst[:, :], acc[:, q * C:(q + 1) * C],
                            x_res[:, i * C:(i + 1) * C]),
                            reads=ACC_ALL + [f"xres{i}"],
                            writes=["outst", "kvstg0", "kvstg1", "browf"])
                        rec.dma("dsp", lambda e, i=i: e.dma_start(
                            out=out_d[i], in_=outst[:, :]),
                            reads=["outst"], writes=["outd"])
                    else:
                        rec.op("dve", lambda v, i=i, q=q: v.tensor_add(
                            x_res[:, i * C:(i + 1) * C], acc[:, q * C:(q + 1) * C],
                            x_res[:, i * C:(i + 1) * C]),
                            reads=ACC_ALL + [f"xres{i}"], writes=[f"xres{i}"])

        subs = [("attn", "a"), ("ffn", 1), ("attn", "b"), ("ffn", 2)][:n_sub]
        for si, (kind, arg) in enumerate(subs):
            is_last = si == len(subs) - 1
            if kind == "attn":
                if arg == "b":
                    rec.pump_all()
                attn(arg, is_last)
            else:
                rec.pump_all()
                if arg == 1 and n_sub >= 3:
                    rec.prep_q.extend(attn_b_items())
                ffn(is_last, PUMP_B and arg == 1 and n_sub >= 3)
        rec.pump_all()
        rec.wait_only("sp", {"dsp": rec.dcnt["dsp"]})
        rec.wait_only("pool", {"dsp": rec.dcnt["dsp"], "dact": rec.dcnt["dact"]})

        waited = {}
        for ename in ENGS:
            for waits, fn, sem, val, dbg in rec.ops[ename]:
                for s, v in waits:
                    waited.setdefault(s, set()).add(v)
        PRUNE = ("pe",)
        remap = {}
        for s, vals in waited.items():
            if s in PRUNE:
                remap[s] = {v: i + 1 for i, v in enumerate(sorted(vals))}
            else:
                remap[s] = {v: v for v in vals}

        with nc.Block() as block, contextlib.ExitStack() as semctx:
            sems = {}
            for s in ("pe", "act", "dve", "pool", "sp", "dsp", "dact", "dpool", "cc"):
                sems[s] = semctx.enter_context(nc.semaphore("sem_" + s))

            def replay(ename, eng):
                for idx, (waits, fn, sem, val, dbg) in enumerate(rec.ops[ename]):
                    for s, v in waits:
                        eng.wait_ge(sems[s], remap[s][v])
                    if fn is not None:
                        try:
                            inst = fn(eng)
                        except Exception as ex:
                            raise RuntimeError(
                                f"replay fail {ename} op#{idx} dbg={dbg}: {ex}") from ex
                        if sem not in PRUNE:
                            inst.then_inc(sems[sem], 16 if sem in QUEUES else 1)
                        elif sem in remap and val in remap[sem]:
                            inst.then_inc(sems[sem], 1)

            @block.tensor
            def _(t):
                replay("pe", t)

            @block.scalar
            def _(s):
                replay("act", s)

            @block.vector
            def _(v):
                replay("dve", v)

            @block.gpsimd
            def _(g):
                replay("pool", g)

            @block.sync
            def _(sy):
                replay("sp", sy)

    return nc


def make_inputs(core, inputs):
    b, h = core // 2, core % 2
    x = np.ascontiguousarray(np.asarray(inputs["L0_attn_a_x"])[b].reshape(32, 128, C)[h::2])
    tri = np.where(np.arange(128)[None, :] <= np.arange(128)[:, None], 0.0,
                   NEG).astype(np.float32)
    if h == 0:
        mask = np.concatenate([tri, np.full((128, 128), NEG, np.float32)], axis=1)
    else:
        mask = np.concatenate([np.zeros((128, 128), np.float32), tri], axis=1)
    d = {
        "x": x,
        "aWQ": inputs["L0_attn_a_WQ_w"], "aWQb": inputs["L0_attn_a_WQ_b"],
        "aWK": inputs["L0_attn_a_WK_w"], "aWKb": inputs["L0_attn_a_WK_b"],
        "aWV": inputs["L0_attn_a_WV_w"], "aWVb": inputs["L0_attn_a_WV_b"],
        "aWO": inputs["L0_attn_a_WO_w"], "aWOb": inputs["L0_attn_a_WO_b"],
        "bwq": inputs["L0_attn_b_wq"], "bwk": inputs["L0_attn_b_wk"],
        "bwv": inputs["L0_attn_b_wv"], "bwo": inputs["L0_attn_b_wo"],
        "bbo": inputs["L0_attn_b_bo"],
        "gw": inputs["L0_ffn_gate_w"],
        "gb_row": np.tile(np.asarray(inputs["L0_ffn_gate_bias"], np.float32),
                          16)[None, :],
        "sgw": inputs["L0_ffn_shared_gw"], "sdw": inputs["L0_ffn_shared_dw"],
        "upw": inputs["L0_ffn_up_w"], "dnw": inputs["L0_ffn_down_w"],
        "idn": np.eye(128, dtype=ml_dtypes.bfloat16),
        "ones_bf": np.ones((1, 128), dtype=ml_dtypes.bfloat16),
        "ones_f": np.ones((1, 128), dtype=np.float32),
        "mask2": mask,
    }
    return {k: np.ascontiguousarray(np.asarray(v)) for k, v in d.items()}


def kernel(**inputs):
    nc = build_nc(4)
    in_maps = [make_inputs(c, inputs) for c in range(8)]
    res = run_bass_kernel_spmd(nc, in_maps, core_ids=list(range(8)))
    out = np.empty((4, 4096, C), dtype=np.float32)
    for c in range(8):
        b, h = c // 2, c % 2
        out[b].reshape(32, 128, C)[h::2] = res.results[c]["out"]
    return out


# version 06
# speedup vs baseline: 1.2241x; 1.2017x over previous
import contextlib
from collections import deque

import numpy as np
import ml_dtypes
import concourse.bass as bass
import concourse.mybir as mybir
from concourse.bass_utils import run_bass_kernel_spmd

F32 = mybir.dt.float32
BF16 = mybir.dt.bfloat16
AF = mybir.ActivationFunctionType
ALU = mybir.AluOpType
AX = mybir.AxisListType

C = 2048
NQT = 16
NH = 16
HS = 2816
HE = 1408
NE = 8
SCALE = 1.0 / float(np.sqrt(128.0))
NEG = -1.0e30
EPS = 1e-5

PUMP_B = True
ENGS = ("pe", "act", "dve", "pool", "sp")
QUEUES = {"dsp": "sp", "dact": "act", "dpool": "pool"}

WB_ALL = ["wbA", "wbB", "wbSS", "dn0", "dn1", "ustg0", "ustg1"]


class Rec:

    def __init__(self):
        self.ops = {e: [] for e in ENGS}
        self.cnt = {e: 0 for e in ENGS}
        self.dcnt = {q: 0 for q in QUEUES}
        self.ccnt = 0
        self.know = {e: {} for e in ENGS}
        self.reg = {}
        self.prep_q = deque()

    def _region(self, r):
        if r not in self.reg:
            self.reg[r] = {"w": {}, "r": {}}
        return self.reg[r]

    def _emit(self, eng, fn, reads, writes, sem, val, skip):
        waits = {}
        for r in reads:
            for s, v in self._region(r)["w"].items():
                if s not in skip and self.know[eng].get(s, 0) < v:
                    waits[s] = max(waits.get(s, 0), v)
        for r in writes:
            rr = self._region(r)
            for s, v in rr["w"].items():
                if s not in skip and self.know[eng].get(s, 0) < v:
                    waits[s] = max(waits.get(s, 0), v)
            for s, v in rr["r"].items():
                if s not in skip and self.know[eng].get(s, 0) < v:
                    waits[s] = max(waits.get(s, 0), v)
        for s, v in waits.items():
            self.know[eng][s] = v
        self.ops[eng].append((sorted(waits.items()), fn, sem, val, (list(reads), list(writes))))
        for r in writes:
            rr = self._region(r)
            rr["w"] = {sem: val}
            rr["r"] = {}
        for r in reads:
            self._region(r)["r"][sem] = val

    def op(self, eng, fn, reads=(), writes=()):
        self.cnt[eng] += 1
        self._emit(eng, fn, reads, writes, eng, self.cnt[eng],
                   (eng,) if eng == "pe" else ())

    def dma(self, q, fn, reads=(), writes=()):
        self.dcnt[q] += 16
        self._emit(QUEUES[q], fn, reads, writes, q, self.dcnt[q], ())

    def coll(self, fn, reads=(), writes=()):
        self.ccnt += 1
        self._emit("pool", fn, reads, writes, "cc", self.ccnt, ())

    def wait_only(self, eng, sems):
        waits = []
        for s, v in sems.items():
            if v > 0 and self.know[eng].get(s, 0) < v:
                waits.append((s, v))
                self.know[eng][s] = v
        if waits:
            self.ops[eng].append((waits, None, None, 0, ([], [])))

    def barrier(self):
        engines = ("pe", "act", "dve", "sp")
        cur = {e: self.cnt[e] for e in engines}
        cur["dsp"] = self.dcnt["dsp"]
        for e in engines:
            self.wait_only(e, {s: v for s, v in cur.items() if s != e})

    def pump(self, n):
        for _ in range(n):
            if not self.prep_q:
                return
            self.prep_q.popleft()()

    def pump_all(self):
        while self.prep_q:
            self.prep_q.popleft()()


def build_nc(n_sub=4):
    nc = bass.Bass("TRN2", target_bir_lowering=False)
    rec = Rec()

    x_in = nc.dram_tensor("x", [NQT, 128, C], F32, kind="ExternalInput").ap()
    out_d = nc.dram_tensor("out", [NQT, 128, C], F32, kind="ExternalOutput").ap()
    aW = {nm: nc.dram_tensor(nm, [C, C], F32, kind="ExternalInput").ap()
          for nm in ("aWQ", "aWK", "aWV", "aWO")}
    ab = {nm: nc.dram_tensor(nm, [C], F32, kind="ExternalInput").ap()
          for nm in ("aWQb", "aWKb", "aWVb", "aWOb")}
    bW = {nm: nc.dram_tensor(nm, [C, C], F32, kind="ExternalInput").ap()
          for nm in ("bwq", "bwk", "bwv", "bwo")}
    bbo = nc.dram_tensor("bbo", [C], F32, kind="ExternalInput").ap()
    gw_in = nc.dram_tensor("gw", [NE, C], F32, kind="ExternalInput").ap()
    gb_row = nc.dram_tensor("gb_row", [1, 128], F32, kind="ExternalInput").ap()
    sgw_in = nc.dram_tensor("sgw", [2 * HS, C], F32, kind="ExternalInput").ap()
    sdw_in = nc.dram_tensor("sdw", [C, HS], F32, kind="ExternalInput").ap()
    upw_in = nc.dram_tensor("upw", [NE, 2 * HE, C], F32, kind="ExternalInput").ap()
    dnw_in = nc.dram_tensor("dnw", [NE, C, HE], F32, kind="ExternalInput").ap()
    idn_in = nc.dram_tensor("idn", [128, 128], BF16, kind="ExternalInput").ap()
    onesb_in = nc.dram_tensor("ones_bf", [1, 128], BF16, kind="ExternalInput").ap()
    onesf_in = nc.dram_tensor("ones_f", [1, 128], F32, kind="ExternalInput").ap()
    mask_in = nc.dram_tensor("mask2", [128, 256], F32, kind="ExternalInput").ap()

    scr = {}
    for nm in ("aWQ", "aWK"):
        scr[nm] = nc.dram_tensor("s_" + nm, [16, 128, C], BF16).ap()
    for nm in ("aWV", "aWO", "bwq", "bwk", "bwv", "bwo"):
        scr[nm] = nc.dram_tensor("s_" + nm, [C, C], BF16).ap()
    scr["sgw"] = nc.dram_tensor("s_sgw", [2 * HS // 128, 128, C], BF16).ap()
    scr["sdw"] = nc.dram_tensor("s_sdw", [HS, C], BF16).ap()
    for e in range(NE):
        scr[f"up{e}"] = nc.dram_tensor(f"s_up{e}", [2 * HE // 128, 128, C], BF16).ap()
        scr[f"dn{e}"] = nc.dram_tensor(f"s_dn{e}", [HE, C], BF16).ap()
    cc_in = nc.dram_tensor("cc_in", [2, C, C], BF16)
    cc_out = nc.dram_tensor("cc_out", [2, 2, C, C], BF16)
    q_dram = nc.dram_tensor("q_dram", [C, C], BF16).ap()
    ctx_dram = nc.dram_tensor("ctx_dram", [C, C], BF16).ap()

    ctx = contextlib.ExitStack()
    sb = lambda name, shape, dt: ctx.enter_context(nc.sbuf_tensor(name, shape, dt))

    with ctx:
        x_res = sb("x_res", [128, NQT * C], BF16)
        arena = sb("arena", [128, NQT * C], BF16)
        ar32 = arena[:, :].bitcast(F32)

        acc = sb("acc", [128, 4 * C], F32)
        h_sb = sb("h_sb", [128, 11 * 512], BF16)
        wbuf = sb("wbuf", [128, 8192], BF16)
        outst = sb("outst", [128, C], F32)
        xnt = wbuf[:, 6144:8192]
        gwb = wbuf[0:8, 0:C]
        outst_bf = outst[:, :].bitcast(BF16)
        idn = sb("sb_idn", [128, 128], BF16)
        mask2 = sb("sb_mask2", [128, 256], F32)
        stats = sb("stats", [128, 96], F32)
        nmx = sb("nmx", [128, 4], F32)
        sall = sb("sall", [128, 128], F32)
        weall = sb("weall", [128, 128], F32)
        rt = sb("rt", [128, 384], F32)
        seg = sb("seg", [128, 64], F32)
        bqc = sb("bqc", [128, 16], F32)
        bkc = sb("bkc", [128, 16], F32)
        brows = sb("brows", [128, C], BF16)
        brow_f = outst[0:1, 0:C]
        brow_bf = brows[0:1, :]
        ones_bf = sb("sb_onesb", [128, 128], BF16)[0:1, :]
        ones_f = sb("sb_onesf", [128, 128], F32)[0:1, :]
        gb_bc = sb("gb_bc", [128, 128], F32)
        consts = sb("consts", [128, 4], F32)
        gwT = sb("gwT", [128, 128], BF16)

        psum = {}
        for nm in ("ps0", "ps1", "psc0", "psc1"):
            psum[nm] = ctx.enter_context(nc.psum_tensor(nm, [128, 512], F32))
        for nm in ("pstr0", "pstr1", "psp0", "psp1"):
            psum[nm] = ctx.enter_context(nc.psum_tensor(nm, [128, 1024], BF16))

        acc_bf = acc[:, :].bitcast(BF16)
        stgf = [acc[:, 0:2048], acc[:, 2048:4096]]
        stgb = [acc_bf[:, 8192:10240], acc_bf[:, 10240:12288]]
        sstgf = outst[:, :]
        sstgb = brows[:, :]

        ACC_ALL = ["acc", "stgf0", "stgf1", "stgb0", "stgb1"]

        def mm(bname, out_ap, lhsT, rhs, start, stop, reads):
            rec.op("pe", lambda t, o=out_ap, l=lhsT, r=rhs, s=start, p=stop:
                   t.matmul(o, l, r, start=s, stop=p),
                   reads=list(reads), writes=[bname])

        def tr(bname, out_ap, in_ap, reads, np_=128):
            rec.op("pe", lambda t, o=out_ap, i=in_ap, n=np_:
                   t.transpose(o, i, idn[0:n, 0:n]),
                   reads=list(reads) + ["idn"], writes=[bname])

        rec.op("dve", lambda v: v.memset(consts[:, 1:2], SCALE), writes=["consts"])
        rec.dma("dsp", lambda e: e.dma_start(out=idn[:, :], in_=idn_in), writes=["idn"])
        rec.dma("dsp", lambda e: e.dma_start(out=mask2[:, :], in_=mask_in), writes=["mask2"])
        rec.dma("dsp", lambda e: e.dma_start(out=ones_bf, in_=onesb_in), writes=["onesb"])
        rec.dma("dsp", lambda e: e.dma_start(out=ones_f, in_=onesf_in), writes=["onesf"])
        rec.dma("dsp", lambda e: e.dma_start(
            out=bqc[:, :], in_=ab["aWQb"].rearrange("(k p) -> p k", p=128),
            allow_slow_non_contiguous=True), writes=["bqc"])
        rec.dma("dsp", lambda e: e.dma_start(
            out=bkc[:, :], in_=ab["aWKb"].rearrange("(k p) -> p k", p=128),
            allow_slow_non_contiguous=True), writes=["bkc"])
        def fill_brow(src_ap):
            rec.dma("dsp", lambda e, s=src_ap: e.dma_start(
                out=brow_f, in_=s.rearrange("(o c) -> o c", o=1)),
                writes=["browf", "outst", "kvstg0", "kvstg1"])
            rec.op("dve", lambda v: v.tensor_copy(brow_bf, brow_f),
                   reads=["browf"], writes=["browbf"])
        rec.dma("dsp", lambda e: e.dma_start(out=brow_f[:, 0:128], in_=gb_row),
                writes=["browf", "outst", "kvstg0", "kvstg1"])
        rec.op("pe", lambda t: t.matmul(psum["ps0"][:, 0:128], ones_f,
                                        brow_f[:, 0:128], start=True, stop=True),
               reads=["onesf", "browf"], writes=["ps0"])
        rec.op("dve", lambda v: v.tensor_copy(gb_bc[:, :], psum["ps0"][:, 0:128]),
               reads=["ps0"], writes=["gbbc"])
        rec.dma("dsp", lambda e: e.dma_start(out=stgf[0][0:8, 0:C], in_=gw_in),
                writes=["stgf0"])
        rec.op("pool", lambda g: g.tensor_copy(gwb[:, :], stgf[0][0:8, 0:C]),
               reads=["stgf0"], writes=["gwb", "wbA"])
        for k in range(16):
            tr("pstr0", psum["pstr0"][:, k * 8:(k + 1) * 8],
               gwb[0:8, k * 128:(k + 1) * 128], ["gwb", "wbA"], np_=8)
        rec.op("act", lambda s: s.copy(out=gwT[:, :], in_=psum["pstr0"][:, 0:128]),
               reads=["pstr0"], writes=["gwT"])

        for i in range(NQT):
            rec.dma("dsp", lambda e, i=i: e.dma_start(out=outst[:, :], in_=x_in[i]),
                    writes=["outst", "kvstg0", "kvstg1", "browf"])
            rec.op("dve", lambda v, i=i: v.tensor_copy(
                x_res[:, i * C:(i + 1) * C], outst[:, :]),
                reads=["outst"], writes=[f"xres{i}"])

        def prep_t3_item(w_ap, scr3_ap, scrkey, fi, slot):
            f_ap, b_ap = stgf[slot], stgb[slot]
            fkeys = [f"stgf{slot}"]
            bkeys = [f"stgb{slot}"]
            rec.dma("dpool", lambda e, w=w_ap, fi=fi, f=f_ap: e.dma_start(
                out=f[:, :], in_=w[fi * 128:(fi + 1) * 128, :]), writes=fkeys)
            rec.op("pool", lambda g, f=f_ap, b=b_ap: g.tensor_copy(
                b[:, :], f[:, :]), reads=fkeys, writes=bkeys)
            for g in range(2):
                bn = f"psp{g}"
                for t in range(8):
                    k = g * 8 + t
                    tr(bn, psum[bn][:, t * 128:(t + 1) * 128],
                       b_ap[:, k * 128:(k + 1) * 128], bkeys)
                rec.op("act", lambda s, bn=bn, a=g * 1024, b=b_ap:
                       s.copy(out=b[:, a:a + 1024], in_=psum[bn][:, :]),
                       reads=[bn], writes=bkeys)
            rec.dma("dpool", lambda e, sc=scr3_ap, fi=fi, b=b_ap: e.dma_start(
                out=sc[fi], in_=b[:, :]), reads=bkeys, writes=[scrkey])

        def prep_col_item(w_ap, scr_ap, scrkey, F, k, slot):
            f_ap, b_ap = stgf[slot], stgb[slot]
            fkeys = [f"stgf{slot}"]
            bkeys = [f"stgb{slot}"]
            rec.dma("dact", lambda e, w=w_ap, k=k, f=f_ap, n=F: e.dma_start(
                out=f[:, 0:n].rearrange("p (fc c) -> p fc c", c=128),
                in_=w[:, k * 128:(k + 1) * 128].rearrange("(fc p) c -> p fc c", p=128)),
                writes=fkeys)
            rec.op("pool", lambda g, f=f_ap, b=b_ap, n=F: g.tensor_copy(
                b[:, 0:n], f[:, 0:n]), reads=fkeys, writes=bkeys)
            for g in range((F // 128 + 7) // 8):
                t0, t1 = g * 8, min(g * 8 + 8, F // 128)
                bn = f"psp{g % 2}"
                for t in range(t0, t1):
                    tr(bn, psum[bn][:, (t - t0) * 128:(t - t0 + 1) * 128],
                       b_ap[:, t * 128:(t + 1) * 128], bkeys)
                rec.op("act", lambda s, bn=bn, a=t0 * 128, w=(t1 - t0) * 128, b=b_ap:
                       s.copy(out=b[:, a:a + w], in_=psum[bn][:, 0:w]),
                       reads=[bn], writes=bkeys)
            rec.dma("dpool", lambda e, sc=scr_ap, k=k, b=b_ap, n=F: e.dma_start(
                out=sc[k * 128:(k + 1) * 128, :], in_=b[:, 0:n]),
                reads=bkeys, writes=[scrkey])

        def prep_natural_item(w_ap, scr_ap, scrkey, F, k):
            rec.dma("dpool", lambda e, w=w_ap, k=k: e.dma_start(
                out=sstgf[:, 0:F], in_=w[k * 128:(k + 1) * 128, :]),
                writes=["outst", "kvstg0", "kvstg1", "browf"])
            rec.op("pool", lambda g: g.tensor_copy(
                sstgb[:, 0:F], sstgf[:, 0:F]), reads=["outst"], writes=["browbf"])
            rec.dma("dpool", lambda e, sc=scr_ap, k=k: e.dma_start(
                out=sc[k * 128:(k + 1) * 128, :], in_=sstgb[:, 0:F]),
                reads=["browbf"], writes=[scrkey])

        def attn_a_items():
            its = []
            ns = [0]
            for nm in ("aWQ", "aWK"):
                for fi in range(16):
                    its.append(lambda nm=nm, fi=fi, s=ns[0] % 2: prep_t3_item(
                        aW[nm], scr[nm], "scr_" + nm, fi, s))
                    ns[0] += 1
            for nm in ("aWV", "aWO"):
                for k in range(16):
                    its.append(lambda nm=nm, k=k, s=ns[0] % 2: prep_col_item(
                        aW[nm], scr[nm], "scr_" + nm, C, k, s))
                    ns[0] += 1
            return its

        def ffn_items():
            its = []
            ns = [0]
            for fi in range(2 * HS // 128):
                its.append(lambda fi=fi, s=ns[0] % 2: prep_t3_item(
                    sgw_in, scr["sgw"], "scr_sgw", fi, s))
                ns[0] += 1
            for k in range(HS // 128):
                its.append(lambda k=k, s=ns[0] % 2: prep_col_item(
                    sdw_in, scr["sdw"], "scr_sdw", C, k, s))
                ns[0] += 1
            for e in range(NE):
                for fi in range(2 * HE // 128):
                    its.append(lambda e=e, fi=fi, s=ns[0] % 2: prep_t3_item(
                        upw_in[e], scr[f"up{e}"], f"scr_up{e}", fi, s))
                    ns[0] += 1
                for k in range(HE // 128):
                    its.append(lambda e=e, k=k, s=ns[0] % 2: prep_col_item(
                        dnw_in[e], scr[f"dn{e}"], f"scr_dn{e}", C, k, s))
                    ns[0] += 1
            return its

        def attn_b_items():
            its = []
            for nm in ("bwq", "bwk", "bwv", "bwo"):
                for k in range(16):
                    its.append(lambda nm=nm, k=k: prep_natural_item(
                        bW[nm], scr[nm], "scr_" + nm, C, k))
            return its

        rec.prep_q.extend(attn_a_items())
        rec.pump_all()
        if n_sub >= 2:
            rec.prep_q.extend(ffn_items())

        def ln_phase():
            for i in range(NQT):
                xr = x_res[:, i * C:(i + 1) * C]
                rec.op("dve", lambda v, x=xr, i=i: v.reduce_sum(
                    stats[:, i:i + 1], x, axis=AX.X),
                    reads=[f"xres{i}"], writes=["st_sum"])
                rec.op("act", lambda s, x=xr, i=i: s.activation(
                    out=xnt[:, :], in_=x, func=AF.Square,
                    accum_out=stats[:, 16 + i:17 + i]),
                    reads=[f"xres{i}"], writes=["xnt", "ustg0", "ustg1", "st_sq"])
            rec.op("dve", lambda v: v.tensor_scalar_mul(
                stats[:, 32:48], stats[:, 0:16], 1.0 / C),
                reads=["st_sum"], writes=["st_mean"])
            rec.op("dve", lambda v: v.tensor_scalar_mul(
                stats[:, 48:64], stats[:, 16:32], 1.0 / C),
                reads=["st_sq"], writes=["st_var"])
            rec.op("dve", lambda v: v.tensor_mul(
                stats[:, 0:16], stats[:, 32:48], stats[:, 32:48]),
                reads=["st_mean"], writes=["st_sum"])
            rec.op("dve", lambda v: v.tensor_sub(
                stats[:, 48:64], stats[:, 48:64], stats[:, 0:16]),
                reads=["st_sum", "st_var"], writes=["st_var"])
            rec.op("dve", lambda v: v.tensor_scalar_add(
                stats[:, 48:64], stats[:, 48:64], EPS),
                reads=["st_var"], writes=["st_var"])
            rec.op("act", lambda s: s.activation(
                out=stats[:, 48:64], in_=stats[:, 48:64], func=AF.Ln),
                reads=["st_var"], writes=["st_var"])
            rec.op("dve", lambda v: v.tensor_scalar_mul(
                stats[:, 48:64], stats[:, 48:64], -0.5),
                reads=["st_var"], writes=["st_var"])
            rec.op("act", lambda s: s.activation(
                out=stats[:, 64:80], in_=stats[:, 48:64], func=AF.Exp),
                reads=["st_var"], writes=["st_rstd"])
            rec.op("dve", lambda v: v.scalar_tensor_tensor(
                stats[:, 80:96], stats[:, 32:48], -1.0, stats[:, 64:80],
                op0=ALU.mult, op1=ALU.mult),
                reads=["st_mean", "st_rstd"], writes=["st_nmr"])
            for i in range(NQT):
                xr = x_res[:, i * C:(i + 1) * C]
                rec.op("act", lambda s, x=xr, i=i: s.activation(
                    out=xnt[:, :], in_=x, func=AF.Identity,
                    bias=stats[:, 80 + i:81 + i], scale=stats[:, 64 + i:65 + i]),
                    reads=[f"xres{i}", "st_rstd", "st_nmr"], writes=["xnt", "ustg0", "ustg1"])
                for g in range(2):
                    bn = f"pstr{g}"
                    for t in range(8):
                        k = g * 8 + t
                        tr(bn, psum[bn][:, t * 128:(t + 1) * 128],
                           xnt[:, k * 128:(k + 1) * 128], ["xnt"])
                    rec.op("act", lambda s, bn=bn, g=g, i=i: s.copy(
                        out=arena[:, :].rearrange("p (k t) -> p k t", k=16)
                        [:, g * 8:(g + 1) * 8, i * 128:(i + 1) * 128],
                        in_=psum[bn][:, :].rearrange("p (k t) -> p k t", k=8)),
                        reads=[bn], writes=["xnf"])

        def load_lhsT_set(dst_ap, dstkeys, scr3_ap, fc):
            rec.dma("dsp", lambda e, d=dst_ap, s=scr3_ap, fc=fc: e.dma_start(
                out=d, in_=s[fc]), writes=list(dstkeys))

        def load_lhsT_set_plain(dst_ap, dstkeys, scr_ap, fc):
            rec.dma("dsp", lambda e, d=dst_ap, s=scr_ap, fc=fc: e.dma_start(
                out=d.rearrange("p (k c) -> p k c", c=128),
                in_=s.rearrange("(k p) f -> p k f", p=128)[:, :, fc * 128:(fc + 1) * 128]),
                writes=list(dstkeys))

        def load_set_slab(dst_ap, dstkeys, scr_ap, f0, w):
            rec.dma("dsp", lambda e, d=dst_ap, s=scr_ap, f0=f0, w=w: e.dma_start(
                out=d.rearrange("p (k c) -> p k c", c=w),
                in_=s.rearrange("(k p) f -> p k f", p=128)[:, :, f0:f0 + w]),
                writes=list(dstkeys))

        def attn(kind, last):
            scr_q = scr["aWQ" if kind == "a" else "bwq"]
            scr_k = scr["aWK" if kind == "a" else "bwk"]
            scr_v = scr["aWV" if kind == "a" else "bwv"]
            scr_o = scr["aWO" if kind == "a" else "bwo"]
            kq = "scr_" + ("aWQ" if kind == "a" else "bwq")
            kk = "scr_" + ("aWK" if kind == "a" else "bwk")
            kv = "scr_" + ("aWV" if kind == "a" else "bwv")
            ko = "scr_" + ("aWO" if kind == "a" else "bwo")

            rec.barrier()
            ln_phase()

            kvstg = [outst_bf[:, 0:512], outst_bf[:, 512:1024]]
            wslot = [wbuf[:, 0:2048], wbuf[:, 2048:4096]]

            for proj, s_ap, s_key, bias, dest, destkey in (
                    ("k", scr_k, kk, bkc if kind == "a" else None, cc_in.ap()[0], "ccin"),
                    ("q", scr_q, kq, bqc if kind == "a" else None, q_dram, "qdram")):
                for dc in range(16):
                    sl = dc % 2
                    if kind == "a":
                        load_lhsT_set(wslot[sl], [f"wb{'AB'[sl]}"], s_ap, dc)
                    else:
                        load_lhsT_set_plain(wslot[sl], [f"wb{'AB'[sl]}"], s_ap, dc)
                    for tch in range(4):
                        bn = f"ps{(dc * 4 + tch) % 2}"
                        for k in range(16):
                            mm(bn, psum[bn][:, :], wslot[sl][:, k * 128:(k + 1) * 128],
                               arena[:, k * C + tch * 512:k * C + tch * 512 + 512],
                               k == 0, k == 15, [f"wb{'AB'[sl]}", "xnf", s_key])
                        st = (dc * 4 + tch) % 2
                        if bias is not None:
                            rec.op("act", lambda s, bn=bn, st=st, dc=dc, b=bias:
                                   s.activation(out=kvstg[st][:, :], in_=psum[bn][:, :],
                                                func=AF.Identity, bias=b[:, dc:dc + 1]),
                                   reads=[bn, "bqc", "bkc"],
                                   writes=[f"kvstg{st}", "browf"])
                        else:
                            rec.op("act", lambda s, bn=bn, st=st:
                                   s.copy(out=kvstg[st][:, :], in_=psum[bn][:, :]),
                                   reads=[bn], writes=[f"kvstg{st}", "browf"])
                        rec.dma("dact", lambda e, d=dest, dc=dc, tch=tch, st=st:
                                e.dma_start(out=d[dc * 128:(dc + 1) * 128,
                                                  tch * 512:(tch + 1) * 512],
                                            in_=kvstg[st][:, :]),
                                reads=[f"kvstg{st}"], writes=[destkey])

            if kind == "a":
                fill_brow(ab["aWVb"])
            for ds in range(4):
                load_set_slab(wbuf[:, 0:8192], WB_ALL, scr_v, ds * 512, 512)
                for tt in range(16):
                    bn = f"ps{(ds * 16 + tt) % 2}"
                    for k in range(16):
                        mm(bn, psum[bn][:, :],
                           arena[:, k * C + tt * 128:k * C + tt * 128 + 128],
                           wbuf[:, k * 512:(k + 1) * 512],
                           k == 0, k == 15 and kind != "a", WB_ALL + ["xnf", kv])
                    if kind == "a":
                        mm(bn, psum[bn][:, :], ones_bf,
                           brow_bf[:, ds * 512:(ds + 1) * 512],
                           False, True, ["onesb", "browbf"])
                    st = (ds * 16 + tt) % 2
                    rec.op("act", lambda s, bn=bn, st=st:
                           s.copy(out=kvstg[st][:, :], in_=psum[bn][:, :]),
                           reads=[bn], writes=[f"kvstg{st}", "browf"])
                    rec.dma("dact", lambda e, tt=tt, ds=ds, st=st:
                            e.dma_start(out=cc_in.ap()[1][tt * 128:(tt + 1) * 128,
                                                          ds * 512:(ds + 1) * 512],
                                        in_=kvstg[st][:, :]),
                            reads=[f"kvstg{st}"], writes=["ccin"])

            rec.coll(lambda g: g.collective_compute(
                "AllGather", ALU.bypass,
                replica_groups=[[0, 1], [2, 3], [4, 5], [6, 7]],
                ins=[cc_in.ap().opt()], outs=[cc_out.ap().opt()]),
                reads=["ccin"], writes=["ccout"])

            rec.barrier()

            S_sb = ar32[:, 0:4096]
            P_sb = arena[:, 8192:12288]
            PT_sb = arena[:, 12288:16384]
            Kh = arena[:, 16384:20480]
            Vh = arena[:, 20480:24576]
            Qh = arena[:, 24576:26624]
            ctxh = arena[:, 26624:28672]
            ctxi = [arena[:, 28672:30720], arena[:, 30720:32768]]

            for h in range(NH):
                rec.pump(22)
                dh = h * 128
                rec.dma("dsp", lambda e, dh=dh: e.dma_start(
                    out=Qh[:, :], in_=q_dram[dh:dh + 128, :]),
                    reads=["qdram"], writes=["Qh"])
                for r in range(2):
                    rec.dma("dsp", lambda e, r=r, dh=dh: e.dma_start(
                        out=Kh[:, r * 2048:(r + 1) * 2048],
                        in_=cc_out.ap()[r, 0, dh:dh + 128, :]),
                        reads=["ccout"], writes=["Kh"])
                    rec.dma("dsp", lambda e, r=r, dh=dh: e.dma_start(
                        out=Vh[:, r * 2048:(r + 1) * 2048].rearrange(
                            "p (m d) -> p m d", d=128),
                        in_=cc_out.ap()[r, 1, :, dh:dh + 128].rearrange(
                            "(m p) d -> p m d", p=128)),
                        reads=["ccout"], writes=["Vh"])
                for i in range(NQT):
                    nr = i + 1
                    ncols = 2 * nr * 128
                    nchunk = 0
                    for r in range(2):
                        wt = nr * 128
                        sb0 = r * wt
                        me = wt - 128
                        c0 = 0
                        while c0 < wt:
                            w = min(512, wt - c0)
                            bn = f"ps{nchunk % 2}"
                            mm(bn, psum[bn][:, 0:w], Qh[:, i * 128:(i + 1) * 128],
                               Kh[:, r * 2048 + c0:r * 2048 + c0 + w],
                               True, True, ["Qh", "Kh"])
                            if c0 + w <= me:
                                rec.op("act", lambda s, bn=bn, d0=sb0 + c0, w=w: s.copy(
                                    out=S_sb[:, d0:d0 + w], in_=psum[bn][:, 0:w]),
                                    reads=[bn], writes=["Ssb"])
                            else:
                                pw = me - c0
                                if pw > 0:
                                    rec.op("act", lambda s, bn=bn, d0=sb0 + c0, pw=pw:
                                           s.copy(out=S_sb[:, d0:d0 + pw],
                                                  in_=psum[bn][:, 0:pw]),
                                           reads=[bn], writes=["Ssb"])
                                rec.op("dve", lambda v, bn=bn, d0=sb0 + me, pw=max(pw, 0),
                                       r=r: v.tensor_add(
                                           S_sb[:, d0:d0 + 128],
                                           psum[bn][:, pw:pw + 128],
                                           mask2[:, r * 128:(r + 1) * 128]),
                                       reads=[bn, "mask2"], writes=["Ssb"])
                            c0 += w
                            nchunk += 1
                    rec.op("dve", lambda v, n=ncols: v.reduce_max(
                        nmx[:, 0:1], S_sb[:, 0:n], axis=AX.X, negate=True),
                        reads=["Ssb"], writes=["nmx"])
                    rec.op("dve", lambda v: v.tensor_scalar_mul(
                        nmx[:, 1:2], nmx[:, 0:1], SCALE), reads=["nmx"], writes=["nmx"])
                    rec.op("act", lambda s, n=ncols: s.activation(
                        out=P_sb[:, 0:n], in_=S_sb[:, 0:n], func=AF.Exp,
                        bias=nmx[:, 1:2], scale=consts[:, 1:2],
                        accum_out=nmx[:, 2:3]),
                        reads=["Ssb", "nmx", "consts"], writes=["Psb", "nmx"])
                    rec.op("dve", lambda v: v.reciprocal(nmx[:, 3:4], nmx[:, 2:3]),
                           reads=["nmx"], writes=["nmx"])
                    rec.op("dve", lambda v, n=ncols: v.tensor_scalar_mul(
                        P_sb[:, 0:n], P_sb[:, 0:n], nmx[:, 3:4]),
                        reads=["nmx", "Psb"], writes=["Psb"])
                    nt = ncols // 128
                    for g in range((nt + 7) // 8):
                        t0, t1 = g * 8, min(g * 8 + 8, nt)
                        bn = f"pstr{g % 2}"
                        for t in range(t0, t1):
                            tr(bn, psum[bn][:, (t - t0) * 128:(t - t0 + 1) * 128],
                               P_sb[:, t * 128:(t + 1) * 128], ["Psb"])
                        rec.op("act", lambda s, bn=bn, t0=t0, w=(t1 - t0) * 128:
                               s.copy(out=PT_sb[:, t0 * 128:t0 * 128 + w],
                                      in_=psum[bn][:, 0:w]),
                               reads=[bn], writes=["PTsb"])
                    cn = f"psc{i % 2}"
                    for t in range(nt):
                        r, m = (0, t) if t < nr else (1, t - nr)
                        vcol = r * 16 + m
                        mm(cn, psum[cn][:, 0:128], Vh[:, vcol * 128:(vcol + 1) * 128],
                           PT_sb[:, t * 128:(t + 1) * 128], t == 0, t == nt - 1,
                           ["Vh", "PTsb"])
                    rec.op("dve", lambda v, cn=cn, i=i: v.tensor_copy(
                        ctxh[:, i * 128:(i + 1) * 128], psum[cn][:, 0:128]),
                        reads=[cn], writes=["ctxh"])
                rec.dma("dact", lambda e, dh=dh: e.dma_start(
                    out=ctx_dram[dh:dh + 128, :], in_=ctxh[:, :]),
                    reads=["ctxh"], writes=["ctxd"])

            fill_brow(ab["aWOb"] if kind == "a" else bbo)
            for fs in range(4):
                load_set_slab(wbuf[:, 0:8192], WB_ALL, scr_o, fs * 512, 512)
                for i in range(NQT):
                    sl = i % 2
                    rec.dma("dsp", lambda e, i=i, sl=sl: e.dma_start(
                        out=ctxi[sl].rearrange("p (k c) -> p k c", c=128),
                        in_=ctx_dram[:, i * 128:(i + 1) * 128].rearrange(
                            "(k p) c -> p k c", p=128)),
                        reads=["ctxd"], writes=[f"ctxi{sl}"])
                    bn = f"ps{i % 2}"
                    for k in range(16):
                        mm(bn, psum[bn][:, :], ctxi[sl][:, k * 128:(k + 1) * 128],
                           wbuf[:, k * 512:(k + 1) * 512], k == 0, False,
                           [f"ctxi{sl}"] + WB_ALL + [ko])
                    mm(bn, psum[bn][:, :], ones_bf,
                       brow_bf[:, fs * 512:(fs + 1) * 512], False, True,
                       ["onesb", "browbf"])
                    if last:
                        rec.op("dve", lambda v, bn=bn, i=i, fs=fs: v.tensor_add(
                            outst[:, fs * 512:(fs + 1) * 512], psum[bn][:, :],
                            x_res[:, i * C + fs * 512:i * C + fs * 512 + 512]),
                            reads=[bn, f"xres{i}"],
                            writes=["outst", "kvstg0", "kvstg1", "browf"])
                        rec.dma("dact", lambda e, i=i, fs=fs: e.dma_start(
                            out=out_d[i][:, fs * 512:(fs + 1) * 512],
                            in_=outst[:, fs * 512:(fs + 1) * 512]),
                            reads=["outst"], writes=["outd"])
                    else:
                        rec.op("dve", lambda v, bn=bn, i=i, fs=fs: v.tensor_add(
                            x_res[:, i * C + fs * 512:i * C + fs * 512 + 512],
                            psum[bn][:, :],
                            x_res[:, i * C + fs * 512:i * C + fs * 512 + 512]),
                            reads=[bn, f"xres{i}"], writes=[f"xres{i}"])

        def ffn(last, pump_b):
            rec.barrier()
            ln_phase()

            for i in range(NQT):
                bn = f"ps{i % 2}"
                for k in range(16):
                    mm(bn, psum[bn][:, 0:8],
                       arena[:, k * C + i * 128:k * C + i * 128 + 128],
                       gwT[:, k * 8:(k + 1) * 8], k == 0, k == 15, ["gwT", "xnf"])
                rec.op("act", lambda s, bn=bn, i=i: s.activation(
                    out=sall[:, i * 8:(i + 1) * 8], in_=psum[bn][:, 0:8],
                    func=AF.Sigmoid),
                    reads=[bn], writes=["sall"])
            rec.op("dve", lambda v: v.tensor_add(rt[:, 0:128], sall[:, :], gb_bc[:, :]),
                   reads=["sall", "gbbc"], writes=["rt"])
            rec.op("dve", lambda v: v.reduce_max(
                seg[:, 0:16], rt[:, 0:128].rearrange("p (s e) -> p s e", e=8),
                axis=AX.X), reads=["rt"], writes=["seg"])
            for i in range(NQT):
                rec.op("dve", lambda v, i=i: v.tensor_scalar(
                    rt[:, 256 + i * 8:256 + (i + 1) * 8], rt[:, i * 8:(i + 1) * 8],
                    seg[:, i:i + 1], None, op0=ALU.is_equal),
                    reads=["rt", "seg"], writes=["rt"])
            rec.op("dve", lambda v: v.scalar_tensor_tensor(
                rt[:, 128:256], rt[:, 256:384], NEG, rt[:, 0:128],
                op0=ALU.mult, op1=ALU.add), reads=["rt"], writes=["rt"])
            rec.op("dve", lambda v: v.reduce_max(
                seg[:, 16:32], rt[:, 128:256].rearrange("p (s e) -> p s e", e=8),
                axis=AX.X), reads=["rt"], writes=["seg"])
            for i in range(NQT):
                rec.op("dve", lambda v, i=i: v.tensor_scalar(
                    rt[:, i * 8:(i + 1) * 8], rt[:, 128 + i * 8:128 + (i + 1) * 8],
                    seg[:, 16 + i:17 + i], None, op0=ALU.is_equal),
                    reads=["rt", "seg"], writes=["rt"])
            rec.op("dve", lambda v: v.tensor_add(
                rt[:, 256:384], rt[:, 256:384], rt[:, 0:128]),
                reads=["rt"], writes=["rt"])
            rec.op("dve", lambda v: v.tensor_mul(
                rt[:, 0:128], rt[:, 256:384], sall[:, :]),
                reads=["rt", "sall"], writes=["rt"])
            rec.op("dve", lambda v: v.reduce_sum(
                seg[:, 32:48], rt[:, 0:128].rearrange("p (s e) -> p s e", e=8),
                axis=AX.X), reads=["rt"], writes=["seg"])
            rec.op("dve", lambda v: v.reciprocal(seg[:, 48:64], seg[:, 32:48]),
                   reads=["seg"], writes=["seg"])
            for i in range(NQT):
                rec.op("dve", lambda v, i=i: v.tensor_scalar_mul(
                    weall[:, i * 8:(i + 1) * 8], rt[:, i * 8:(i + 1) * 8],
                    seg[:, 48 + i:49 + i]), reads=["rt", "seg"], writes=["weall"])

            pes = [
                (scr["sgw"], "scr_sgw", [22 + m for m in range(11)], list(range(11)),
                 scr["sdw"], "scr_sdw", list(range(11)), "init", None),
                (scr["sgw"], "scr_sgw", [33 + m for m in range(11)],
                 [11 + m for m in range(11)],
                 scr["sdw"], "scr_sdw", [11 + m for m in range(11)], "add", None),
            ]
            for e in range(NE):
                pes.append((scr[f"up{e}"], f"scr_up{e}", list(range(11)),
                            [11 + m for m in range(11)],
                            scr[f"dn{e}"], f"scr_dn{e}", list(range(11)), "we", e))

            upslot = [wbuf[:, 0:2048], wbuf[:, 2048:4096]]
            dnslot = [wbuf[:, 4096 + 512 * s:4608 + 512 * s] for s in range(4)]
            ustg = [wbuf[:, 6144:6656], wbuf[:, 6656:7168]]

            nslot = 0
            for tc in range(4):
                for ups, upkey, gch, lch, dns, dnkey, dnch, mode, eidx in pes:
                    if pump_b:
                        rec.pump(2)
                    for m in range(11):
                        for half in range(2):
                            hc = gch[m] if half == 0 else lch[m]
                            sl = nslot % 2
                            load_lhsT_set(upslot[sl], [f"wb{'AB'[sl]}"], ups, hc)
                            bn = f"ps{nslot % 2}"
                            for k in range(16):
                                mm(bn, psum[bn][:, :],
                                   upslot[sl][:, k * 128:(k + 1) * 128],
                                   arena[:, k * C + tc * 512:k * C + tc * 512 + 512],
                                   k == 0, k == 15, [f"wb{'AB'[sl]}", "xnf", upkey])
                            if half == 0:
                                rec.op("act", lambda s, bn=bn, m=m: s.activation(
                                    out=h_sb[:, m * 512:(m + 1) * 512],
                                    in_=psum[bn][:, :], func=AF.Silu),
                                    reads=[bn],
                                    writes=[f"h{m}"] + (["stgb0"] if m <= 5 else [])
                                    + (["stgb1"] if m >= 5 else []))
                            else:
                                usl = nslot % 2
                                rec.op("act", lambda s, bn=bn, usl=usl: s.copy(
                                    out=ustg[usl][:, :], in_=psum[bn][:, :]),
                                    reads=[bn], writes=[f"ustg{usl}", "xnt"])
                                rec.op("dve", lambda v, m=m, usl=usl: v.tensor_mul(
                                    h_sb[:, m * 512:(m + 1) * 512],
                                    h_sb[:, m * 512:(m + 1) * 512], ustg[usl][:, :]),
                                    reads=[f"ustg{usl}", f"h{m}"],
                                    writes=[f"h{m}"] + (["stgb0"] if m <= 5 else [])
                                    + (["stgb1"] if m >= 5 else []))
                            nslot += 1
                    dbanks = ["ps0", "ps1", "psc0", "psc1"]
                    for fs in range(4):
                        for m in range(11):
                            sl = m % 4
                            rec.dma("dsp", lambda e, sl=sl, dns=dns, m=m, fs=fs,
                                    dnch=dnch: e.dma_start(
                                        out=dnslot[sl][:, :],
                                        in_=dns[dnch[m] * 128:(dnch[m] + 1) * 128,
                                                fs * 512:(fs + 1) * 512]),
                                    reads=[dnkey], writes=[f"dn{sl}"])
                            for q in range(4):
                                bn = dbanks[q]
                                mm(bn, psum[bn][:, :],
                                   h_sb[:, m * 512 + q * 128:m * 512 + q * 128 + 128],
                                   dnslot[sl][:, :], m == 0, m == 10,
                                   [f"dn{sl}", f"h{m}"])
                        for q in range(4):
                            bn = dbanks[q]
                            i = tc * 4 + q
                            a_sl = acc[:, q * C + fs * 512:q * C + fs * 512 + 512]
                            akeys = ["acc", "stgf0", "stgf1", "stgb0", "stgb1"]
                            if mode == "init":
                                rec.op("dve", lambda v, bn=bn, a=a_sl: v.tensor_copy(
                                    a, psum[bn][:, :]), reads=[bn], writes=akeys)
                            elif mode == "add":
                                rec.op("dve", lambda v, bn=bn, a=a_sl: v.tensor_add(
                                    a, a, psum[bn][:, :]),
                                    reads=[bn] + ACC_ALL, writes=akeys)
                            else:
                                rec.op("dve", lambda v, bn=bn, a=a_sl, i=i, e=eidx:
                                       v.scalar_tensor_tensor(
                                           a, psum[bn][:, :],
                                           weall[:, i * 8 + e:i * 8 + e + 1], a,
                                           op0=ALU.mult, op1=ALU.add),
                                       reads=[bn, "weall"] + ACC_ALL, writes=akeys)
                for q in range(4):
                    i = tc * 4 + q
                    if last:
                        rec.op("dve", lambda v, i=i, q=q: v.tensor_add(
                            outst[:, :], acc[:, q * C:(q + 1) * C],
                            x_res[:, i * C:(i + 1) * C]),
                            reads=ACC_ALL + [f"xres{i}"],
                            writes=["outst", "kvstg0", "kvstg1", "browf"])
                        rec.dma("dact", lambda e, i=i: e.dma_start(
                            out=out_d[i], in_=outst[:, :]),
                            reads=["outst"], writes=["outd"])
                    else:
                        rec.op("dve", lambda v, i=i, q=q: v.tensor_add(
                            x_res[:, i * C:(i + 1) * C], acc[:, q * C:(q + 1) * C],
                            x_res[:, i * C:(i + 1) * C]),
                            reads=ACC_ALL + [f"xres{i}"], writes=[f"xres{i}"])

        subs = [("attn", "a"), ("ffn", 1), ("attn", "b"), ("ffn", 2)][:n_sub]
        for si, (kind, arg) in enumerate(subs):
            is_last = si == len(subs) - 1
            if kind == "attn":
                if arg == "b":
                    rec.pump_all()
                attn(arg, is_last)
            else:
                rec.pump_all()
                if arg == 1 and n_sub >= 3:
                    rec.prep_q.extend(attn_b_items())
                ffn(is_last, PUMP_B and arg == 1 and n_sub >= 3)
        rec.pump_all()
        rec.wait_only("sp", {"dsp": rec.dcnt["dsp"]})
        rec.wait_only("pool", {"dsp": rec.dcnt["dsp"], "dact": rec.dcnt["dact"]})

        waited = {}
        for ename in ENGS:
            for waits, fn, sem, val, dbg in rec.ops[ename]:
                for s, v in waits:
                    waited.setdefault(s, set()).add(v)
        PRUNE = ("pe",)
        remap = {}
        for s, vals in waited.items():
            if s in PRUNE:
                remap[s] = {v: i + 1 for i, v in enumerate(sorted(vals))}
            else:
                remap[s] = {v: v for v in vals}

        with nc.Block() as block, contextlib.ExitStack() as semctx:
            sems = {}
            for s in ("pe", "act", "dve", "pool", "sp", "dsp", "dact", "dpool", "cc"):
                sems[s] = semctx.enter_context(nc.semaphore("sem_" + s))

            def replay(ename, eng):
                for idx, (waits, fn, sem, val, dbg) in enumerate(rec.ops[ename]):
                    for s, v in waits:
                        eng.wait_ge(sems[s], remap[s][v])
                    if fn is not None:
                        try:
                            inst = fn(eng)
                        except Exception as ex:
                            raise RuntimeError(
                                f"replay fail {ename} op#{idx} dbg={dbg}: {ex}") from ex
                        if sem not in PRUNE:
                            inst.then_inc(sems[sem], 16 if sem in QUEUES else 1)
                        elif sem in remap and val in remap[sem]:
                            inst.then_inc(sems[sem], 1)

            @block.tensor
            def _(t):
                replay("pe", t)

            @block.scalar
            def _(s):
                replay("act", s)

            @block.vector
            def _(v):
                replay("dve", v)

            @block.gpsimd
            def _(g):
                replay("pool", g)

            @block.sync
            def _(sy):
                replay("sp", sy)

    return nc


def make_inputs(core, inputs):
    b, h = core // 2, core % 2
    x = np.ascontiguousarray(np.asarray(inputs["L0_attn_a_x"])[b].reshape(32, 128, C)[h::2])
    tri = np.where(np.arange(128)[None, :] <= np.arange(128)[:, None], 0.0,
                   NEG).astype(np.float32)
    if h == 0:
        mask = np.concatenate([tri, np.full((128, 128), NEG, np.float32)], axis=1)
    else:
        mask = np.concatenate([np.zeros((128, 128), np.float32), tri], axis=1)
    d = {
        "x": x,
        "aWQ": inputs["L0_attn_a_WQ_w"], "aWQb": inputs["L0_attn_a_WQ_b"],
        "aWK": inputs["L0_attn_a_WK_w"], "aWKb": inputs["L0_attn_a_WK_b"],
        "aWV": inputs["L0_attn_a_WV_w"], "aWVb": inputs["L0_attn_a_WV_b"],
        "aWO": inputs["L0_attn_a_WO_w"], "aWOb": inputs["L0_attn_a_WO_b"],
        "bwq": inputs["L0_attn_b_wq"], "bwk": inputs["L0_attn_b_wk"],
        "bwv": inputs["L0_attn_b_wv"], "bwo": inputs["L0_attn_b_wo"],
        "bbo": inputs["L0_attn_b_bo"],
        "gw": inputs["L0_ffn_gate_w"],
        "gb_row": np.tile(np.asarray(inputs["L0_ffn_gate_bias"], np.float32),
                          16)[None, :],
        "sgw": inputs["L0_ffn_shared_gw"], "sdw": inputs["L0_ffn_shared_dw"],
        "upw": inputs["L0_ffn_up_w"], "dnw": inputs["L0_ffn_down_w"],
        "idn": np.eye(128, dtype=ml_dtypes.bfloat16),
        "ones_bf": np.ones((1, 128), dtype=ml_dtypes.bfloat16),
        "ones_f": np.ones((1, 128), dtype=np.float32),
        "mask2": mask,
    }
    return {k: np.ascontiguousarray(np.asarray(v)) for k, v in d.items()}


def kernel(**inputs):
    nc = build_nc(4)
    in_maps = [make_inputs(c, inputs) for c in range(8)]
    res = run_bass_kernel_spmd(nc, in_maps, core_ids=list(range(8)))
    out = np.empty((4, 4096, C), dtype=np.float32)
    for c in range(8):
        b, h = c // 2, c % 2
        out[b].reshape(32, 128, C)[h::2] = res.results[c]["out"]
    return out
